# version 01
# baseline (speedup 1.0000x reference)
import os
import numpy as np
from contextlib import ExitStack

import concourse.bass as bass
import concourse.mybir as mybir
from concourse.bass_utils import run_bass_kernel_spmd

F32 = mybir.dt.float32
BF16 = mybir.dt.bfloat16
AF = mybir.ActivationFunctionType
ALU = mybir.AluOpType
AX = mybir.AxisListType
ts = bass.ts

S = 4096
Dm = 1024
NT = 32
NCH = 8
NHEAD = 16
DH = 64
TS = 1024
NSTRIP = S // TS
HR = 3072
HS = 2048
NE = 12
NEG = -1.0e30

_USED = set()


def host_pack(inp):
    P = {}
    C = np.ascontiguousarray

    def lhs_blocks(wT, no):
        K, M = wT.shape
        return C(wT.reshape(K // 128, 128, no, M // no).transpose(2, 1, 0, 3))

    P["wqa"] = C(inp["L0_attn_a_Wq"].T)
    P["wka"] = C(inp["L0_attn_a_Wk"].T)
    P["wva"] = C(inp["L0_attn_a_Wv"].T)
    P["woa"] = lhs_blocks(C(inp["L0_attn_a_Wo"].T), 8)
    for nm, key in [("wqb", "L0_attn_b_wq"), ("wkb", "L0_attn_b_wk"), ("wvb", "L0_attn_b_wv"),
                    ("wgb", "L0_attn_b_wg"), ("wob", "L0_attn_b_wo")]:
        P[nm] = lhs_blocks(C(inp[key].T), 8)
    P["ws1"] = lhs_blocks(inp["L0_ffn_Ws1"], 32)
    P["ws2"] = lhs_blocks(inp["L0_ffn_Ws2"], 8)
    P["we1"] = C(inp["L0_ffn_We1"].reshape(NE, 8, 128, 48, 128).transpose(0, 3, 2, 1, 4))
    perm24 = [(k % 12) * 2 + k // 12 for k in range(24)]
    P["we2"] = C(inp["L0_ffn_We2"].reshape(NE, 24, 128, 8, 128)[:, perm24]
                 .transpose(0, 3, 2, 1, 4))
    P["wr"] = C(inp["L0_ffn_Wr"])
    P["brr"] = C(inp["L0_ffn_br"].reshape(1, 12))
    P["bs1a"] = C(inp["L0_ffn_bs1"][:HS].reshape(16, 128).T)
    P["bs1g"] = C(inp["L0_ffn_bs1"][HS:].reshape(16, 128).T)
    P["bs2r"] = C(inp["L0_ffn_bs2"].reshape(1, Dm))
    a_ = inp["L0_ffn_be1"][:, :HR].reshape(NE, 24, 128).transpose(0, 2, 1)
    g_ = inp["L0_ffn_be1"][:, HR:].reshape(NE, 24, 128).transpose(0, 2, 1)
    be = np.empty((NE, 128, 48), np.float32)
    for sl in range(2):
        for j in range(12):
            be[:, :, sl * 24 + j * 2 + 0] = a_[:, :, 2 * j + sl]
            be[:, :, sl * 24 + j * 2 + 1] = g_[:, :, 2 * j + sl]
    P["be1ag"] = C(be)
    P["be2all"] = C(np.concatenate([inp["L0_ffn_be2"], inp["L0_ffn_bs2"].reshape(1, Dm)], 0).astype(np.float32))


    f32 = np.float32
    inv_freq = (1.0 / 10000.0 ** (np.arange(0, DH, 2, dtype=f32) / f32(DH))).astype(f32)
    t = np.arange(S, dtype=f32)
    freqs = np.einsum("i,j->ij", t, inv_freq).astype(f32)
    emb = np.concatenate([freqs, freqs], -1)
    P["ropec"] = C(np.cos(emb).astype(f32).T)
    sinm = np.sin(emb).astype(f32).T.copy()
    sinm[:32, :] *= -1.0
    P["ropes"] = C(sinm)
    gamma = (1.0 - np.exp2(-5.0 - np.arange(NHEAD, dtype=f32))).astype(f32)
    ii = np.arange(128, dtype=np.float64)
    dtt = np.zeros((128, NHEAD * 128), f32)
    gup = np.zeros((64, NHEAD * 128), f32)
    gkd = np.zeros((128, NHEAD), f32)
    gcc = np.zeros((64, NHEAD), f32)
    for h in range(NHEAD):
        g = np.float64(gamma[h])
        diff = ii[None, :] - ii[:, None]
        dtt[:, h * 128:(h + 1) * 128] = np.where(diff >= 0, g ** np.maximum(diff, 0) / 8.0, 0.0).astype(f32)
        gup[:, h * 128:(h + 1) * 128] = (g ** (ii[None, :] + 1.0)).astype(f32)
        gkd[:, h] = (g ** (127.0 - ii) / 8.0).astype(f32)
        gcc[:, h] = f32(g ** 128.0)
    P["dtt"] = dtt
    P["gup"] = gup
    P["gkd"] = gkd
    P["gcc"] = gcc
    return P


class Prog:
    KEYS = ("QW", "QG", "QV", "XN", "WY", "XP", "WA", "WB", "RT", "GR", "BE", "AV",
            "T", "V", "S", "P", "O")
    DMA_KEYS = ("QW", "QG", "QV", "XN", "WY", "XP", "WA", "WB", "RT", "GR", "BE", "AV", "O")

    def __init__(self, nc, ctx):
        self.nc = nc
        self.ctx = ctx
        self.sem = {k: ctx.enter_context(nc.semaphore("s" + k)) for k in self.KEYS}
        self.base = {k: 0 for k in self.KEYS}
        self.stack = []
        self.ctr = {}
        self.sreg = {}
        self.engs = {}

    def register_engines(self, engs):
        for eng in engs:
            et = eng.engine
            self.engs[et] = eng
            self.sreg[et] = [eng.alloc_register(f"off{j}_{et}") for j in range(2)]
        for (et, k) in sorted(_USED, key=str):
            eng = self.engs[et]
            r = eng.alloc_register(f"c{k}_{et}")
            eng.reg_mov(r, 0)
            self.ctr[(et, k)] = {"reg": r, "lin": 0}

    def L(self, key):
        v = self.base[key]
        for Lp in self.stack:
            v += Lp.local[key]
        return v

    def inc(self, ins, key, n=None):
        n = n if n is not None else (16 if key in self.DMA_KEYS else 1)
        ins.then_inc(self.sem[key], n)
        if self.stack:
            self.stack[-1].local[key] += n
        else:
            self.base[key] += n
        return ins

    def wait(self, eng, key):
        if (eng.engine, key) not in self.ctr:
            _USED.add((eng.engine, key))
            r = eng.alloc_register(f"c{key}_{eng.engine}_x")
            eng.reg_mov(r, 0)
            self.ctr[(eng.engine, key)] = {"reg": r, "lin": self.L(key)}
        c = self.ctr[(eng.engine, key)]
        tgt = self.L(key)
        d = tgt - c["lin"]
        assert d >= 0, f"non-monotone wait {key}"
        if d:
            eng.reg_add(c["reg"], c["reg"], d)
            c["lin"] = tgt
        eng.wait_ge(self.sem[key], c["reg"])

    def dyn(self, eng, tensor, var, coeff, const, pattern):
        if isinstance(var, int):
            const += var * coeff
            var = None
        if var is None:
            return bass.AP(tensor, const, pattern)
        et = eng.engine
        r = self.sreg[et][self._srot.setdefault(et, 0)]
        self._srot[et] = (self._srot[et] + 1) % 2
        h = _var_handle(var, et)
        eng.reg_mov(r, h)
        if coeff != 1:
            eng.reg_mul(r, r, coeff)
        if const:
            eng.reg_add(r, r, const)
        return bass.AP(tensor, r, pattern)

    _srot = {}

    def dyn2(self, eng, tensor, var1, c1, var2, c2, const, pattern):
        if isinstance(var1, int):
            return self.dyn(eng, tensor, var2, c2, const + var1 * c1, pattern)
        if isinstance(var2, int):
            return self.dyn(eng, tensor, var1, c1, const + var2 * c2, pattern)
        et = eng.engine
        r = self.sreg[et][self._srot.setdefault(et, 0)]
        self._srot[et] = (self._srot[et] + 1) % 2
        eng.reg_mov(r, _var_handle(var1, et))
        if c1 != 1:
            eng.reg_mul(r, r, c1)
        r2 = self.sreg[et][self._srot[et]]
        eng.reg_mov(r2, _var_handle(var2, et))
        if c2 != 1:
            eng.reg_mul(r2, r2, c2)
        eng.reg_add(r, r, r2)
        if const:
            eng.reg_add(r, r, const)
        return bass.AP(tensor, r, pattern)

    def loop(self, name, n):
        return _Loop(self, name, n)


def _var_handle(rv, et):
    val = rv.val
    hs = getattr(val, "handles", None)
    if hs is None:
        raise RuntimeError("dyn() needs a raw loop var")
    for h in hs:
        if h.engine == et:
            return h
    raise RuntimeError(f"no handle for {et}")


class _Loop:
    def __init__(self, prog, name, n):
        self.prog = prog
        self.name = name
        self.n = n
        self.local = {k: 0 for k in Prog.KEYS}
        self.touched = set()
        self.entry_lin = {}
        self.cm = None

    def __enter__(self):
        P = self.prog
        nc = P.nc
        self.entryL = {k: P.L(k) for k in Prog.KEYS}
        for (et, k), c in P.ctr.items():
            d = self.entryL[k] - c["lin"]
            assert d >= 0
            if d:
                P.engs[et].reg_add(c["reg"], c["reg"], d)
                c["lin"] = self.entryL[k]
        import concourse.mybir as _mb
        loop_id = nc.next_id()
        ln_ = f"{self.name}_fori_{loop_id}"
        self.loop_start = f"{ln_}_loop"
        self.loop_end = f"{ln_}_end"
        self.regs = nc.alloc_registers(f"{ln_}_i", engines=_mb.ALL_ENGINES)
        nc.regs_mov(self.regs, 0)
        nc.br(self.loop_start, engines=_mb.ALL_ENGINES)
        self.body_cm = nc.body(self.loop_start, valid_engines=_mb.ALL_ENGINES)
        self.body_cm.__enter__()
        self.var = nc.snap(self.regs, min_val=0, max_val=self.n - 1)
        P.stack.append(self)
        return self.var

    def _close_fori(self):
        P = self.prog
        nc = P.nc
        import concourse.mybir as _mb
        nc.regs_alu(self.regs, self.regs, 1, op=_mb.AluOpType.add)
        nc.br_lt(self.regs, self.n, on_true=self.loop_start, on_false=self.loop_end,
                 engines=_mb.ALL_ENGINES)
        self.body_cm.__exit__(None, None, None)
        nc.switch_bb(self.loop_end)
        for rset in (self.regs, getattr(self.var.val, "handles", None) or ()):
            hs = getattr(rset, "handles", None)
            hs = hs() if callable(hs) else (hs if hs is not None else rset)
            for h in hs:
                try:
                    P.engs[h.engine].free_register(h)
                except Exception:
                    pass

    def __exit__(self, *a):
        P = self.prog
        for (et, k), c in sorted(P.ctr.items(), key=str):
            end = self.entryL[k] + self.local[k]
            adj = end - c["lin"]
            assert adj >= 0, (self.name, et, k, adj)
            if adj:
                P.engs[et].reg_add(c["reg"], c["reg"], adj)
            c["lin"] = end
        P.stack.pop()
        tgt = P.stack[-1].local if P.stack else P.base
        for k, v in self.local.items():
            tgt[k] = tgt[k] + v * self.n
        for (et, k), c in P.ctr.items():
            c["lin"] += (self.n - 1) * self.local[k]
        self._close_fori()
        return False


def build(debug=0, phases=9):
    pass
    nc = bass.Bass("TRN2", target_bir_lowering=False)
    ctx = ExitStack()
    sb = lambda name, shape, dt: ctx.enter_context(nc.sbuf_tensor(name, shape, dt))

    def din(name, shape, dt=F32):
        return nc.dram_tensor(name, list(shape), dt, kind="ExternalInput").ap()

    def dscr(name, shape, dt):
        return nc.dram_tensor(name, list(shape), dt, kind="Internal").ap()

    def dout(name, shape, dt=F32):
        return nc.dram_tensor(name, list(shape), dt, kind="ExternalOutput").ap()

    x_in = din("x", [S, Dm])
    wqa = din("wqa", [64, 64]); wka = din("wka", [64, 64]); wva = din("wva", [64, 64])
    woa = din("woa", [8, 128, 8, 128])
    wqb = din("wqb", [8, 128, 8, 128]); wkb = din("wkb", [8, 128, 8, 128])
    wvb = din("wvb", [8, 128, 8, 128]); wgb = din("wgb", [8, 128, 8, 128])
    wob = din("wob", [8, 128, 8, 128])
    ws1 = din("ws1", [32, 128, 8, 128]); ws2 = din("ws2", [8, 128, 16, 128])
    we1 = din("we1", [NE, 48, 128, 8, 128]); we2 = din("we2", [NE, 8, 128, 24, 128])
    wr_in = din("wr", [Dm, 12]); brr = din("brr", [1, 12])
    bs1a = din("bs1a", [128, 16]); bs1g = din("bs1g", [128, 16]); bs2r = din("bs2r", [1, Dm])
    be1ag = din("be1ag", [NE, 128, 48]); be2all = din("be2all", [NE + 1, Dm])
    ropec_in = din("ropec", [64, S]); ropes_in = din("ropes", [64, S])
    dtt_in = din("dtt", [128, NHEAD * 128]); gup_in = din("gup", [64, NHEAD * 128])
    gkd_in = din("gkd", [128, NHEAD]); gcc_in = din("gcc", [64, NHEAD])

    out = dout("out", [S, Dm])

    R1 = dscr("R1", [S, Dm], F32)
    R2 = dscr("R2", [S, Dm], F32)
    xnT1 = dscr("xnT1", [NCH, 128, S], BF16)
    xnT2 = dscr("xnT2", [NCH, 128, S], BF16)
    xnT3 = dscr("xnT3", [NCH, 128, S], BF16)
    xnT4 = dscr("xnT4", [NCH, 128, S], BF16)
    retT_d = dscr("retT_d", [NCH, 128, S], BF16)
    yr_d = dscr("yr_d", [NSTRIP, 128, 8 * TS], F32)
    avT_d = dscr("avT_d", [NCH, 128, S], BF16)
    augrow_d = dscr("augrow_d", [2, 32, 128], BF16)
    wg_d = dscr("wg_d", [16, S], F32)
    wg_d2 = dscr("wg_d2", [16, S], F32)
    wgb_d = dscr("wgb_d", [16, S], BF16)
    wgb_d2 = dscr("wgb_d2", [16, S], BF16)

    scr = dict(xnT1=xnT1, R1=R1, xnT2=xnT2, wg=wg_d, R2=R2, xnT3=xnT3, R3=R1, xnT4=xnT4,
               wg2=wg_d2)
    dbg = {}
    if debug:

        dbg["xnT1"] = dout("dbg_xnT1", [NCH, 128, S], BF16)
        dbg["R1"] = dout("dbg_R1", [S, Dm])
        dbg["xnT2"] = dout("dbg_xnT2", [NCH, 128, S], BF16)
        dbg["wg"] = dout("dbg_wg", [16, S])
        dbg["R2"] = dout("dbg_R2", [S, Dm])
        dbg["xnT3"] = dout("dbg_xnT3", [NCH, 128, S], BF16)
        dbg["wg2"] = dout("dbg_wg2", [16, S])
        dbg["R3"] = dout("dbg_R3", [S, Dm])
        dbg["xnT4"] = dout("dbg_xnT4", [NCH, 128, S], BF16)

    TEN, VEC, SCA, GPS, SYN = nc.tensor, nc.vector, nc.scalar, nc.gpsimd, nc.sync
    P = Prog(nc, ctx)
    P.register_engines([TEN, VEC, SCA, GPS, SYN])

    PB = ctx.enter_context(nc.psum_tensor("PB", [128, 4096], F32))

    def bank(i, w=1):
        return PB[:, 512 * i: 512 * (i + w)]

    def bankb(i, w=1):
        return PB[:, 512 * i: 512 * (i + w)].bitcast(BF16)

    identb = sb("identb", [128, 128], BF16)
    identf = sb("identf", [128, 128], F32)
    maskA = sb("maskA", [128, 128], F32)
    maskMb = sb("maskMb", [128, 128], BF16)
    eps_c = sb("eps_c", [128, 1], F32)
    onesc_f = sb("onesc_f", [1, 128], F32)
    iota16 = sb("iota16", [128, 16], F32)
    wr_sb = sb("wr_sb", [128, 8 * 12], F32)
    br_sb = sb("br_sb", [1, 12], F32)
    Rt = sb("Rt", [128, 4 * Dm], F32)
    sqb = sb("sqb", [128, Dm], F32)
    stt = sb("stt", [128, 4 * 16], F32)
    xnb_t = sb("xnb_t", [128, 2 * Dm], BF16)
    xnT_st = sb("xnT_st", [128, 4 * Dm], BF16)
    gstg2 = sb("gstg2", [16, 4 * 128], F32)
    gstg2b = sb("gstg2b", [16, 4 * 128], BF16)
    BUF = {}
    lgt = sb("lgt", [128, 2 * 16], F32)
    m8b = sb("m8b", [128, 2 * 8], F32)
    gtok = sb("gtok", [128, 2 * 16], F32)
    gmsk = sb("gmsk", [128, 2 * 16], F32)
    gstg = sb("gstg", [16, 2 * 128], F32)
    wstA = sb("wstA", [128, 2 * 2048], F32)
    wbufA = sb("wbufA", [128, 2 * 1024], BF16)
    wbufG = sb("wbufG", [128, 2 * 1024], BF16)
    bstg = sb("bstg", [128, 64], F32)

    for tgt_, fill, pat, cmpo, cm, fv in [
        (identb, 1.0, [[1, 128]], ALU.is_equal, -1, 0.0),
        (identf, 1.0, [[1, 128]], ALU.is_equal, -1, 0.0),
        (maskA, 0.0, [[-1, 128]], ALU.is_ge, 1, NEG),
        (maskMb, 1.0, [[1, 128]], ALU.is_ge, -1, 0.0),
    ]:
        ins = GPS.memset(tgt_[:], fill)
        P.inc(ins, "P")
        P.wait(GPS, "P")
        ins = GPS.affine_select(tgt_[:], tgt_[:], pattern=pat, compare_op=cmpo, fill=fv,
                                base=0, channel_multiplier=cm)
        P.inc(ins, "P")
    ins = GPS.iota(iota16[:], pattern=[[1, 16]], base=0, channel_multiplier=0,
                   allow_small_or_imprecise_dtypes=True)
    P.inc(ins, "P")
    VEC.memset(eps_c[:], 1e-5)
    VEC.memset(onesc_f[:], 1.0)
    VEC.memset(lgt[:], NEG)
    ins = VEC.memset(stt[:], 0.0)
    P.inc(ins, "V")
    ins = SYN.dma_start(out=wr_sb[:].rearrange("p (k m) -> p k m", k=8),
                        in_=wr_in.rearrange("(k p) m -> p k m", p=128))
    P.inc(ins, "QW")
    ins = SYN.dma_start(out=br_sb[:], in_=brr)
    P.inc(ins, "QW")
    P.wait(GPS, "V"); P.wait(GPS, "P")
    P.wait(TEN, "V"); P.wait(TEN, "P")
    P.wait(VEC, "P")
    P.wait(SCA, "V"); P.wait(SCA, "P")

    def emit_boundary_strip(svar, svc, svk, R_src, outT_src, R_dst, xnT_dst, router, do_ln,
                            b_trf, b_trb, b_rt, gate_dst=None, r_dst_is_out=False,
                            gateb_dst=None, gbias=None):
        rpat4 = [[1024, 128], [131072, 4], [1, 1024]]
        P.wait(GPS, "V")
        P.wait(GPS, "QG")
        P.wait(GPS, "O")
        P.wait(VEC, "QG")
        ins = GPS.dma_start(out=Rt[:].rearrange("p (t m) -> p t m", t=4),
                            in_=P.dyn(GPS, R_src.tensor, svar, svc * 524288, svk * 524288, rpat4))
        P.inc(ins, "RT")
        P.wait(VEC, "RT")
        P.wait(SCA, "RT")
        for j in range(4):
            o = j * 16
            rt = Rt[:, j * Dm:(j + 1) * Dm]
            sq = sqb[:, 0:Dm]
            if outT_src is not None:
                for g in range(2):
                    P.wait(TEN, "V")
                    for jj in range(4):
                        c = g * 4 + jj
                        pt = bank(b_trf + g)[:, jj * 128:(jj + 1) * 128]
                        ins = TEN.transpose(pt, outT_src(c, j), identf[:])
                    P.inc(ins, "T")
                    P.wait(VEC, "T")
                    ins = VEC.tensor_add(rt[:, g * 512:(g + 1) * 512], rt[:, g * 512:(g + 1) * 512],
                                         bank(b_trf + g))
                    P.inc(ins, "V")
                if gbias is not None:
                    wgb13, be2b = gbias
                    for g in range(2):
                        P.wait(TEN, "V")
                        ins = TEN.matmul(bank(b_rt)[:, 0:512], wgb13[0:13, j * 128:(j + 1) * 128],
                                         be2b[0:13, g * 512:(g + 1) * 512], start=True, stop=True,
                                         skip_group_check=True)
                        P.inc(ins, "T")
                        P.wait(VEC, "T")
                        ins = VEC.tensor_add(rt[:, g * 512:(g + 1) * 512],
                                             rt[:, g * 512:(g + 1) * 512], bank(b_rt)[:, 0:512])
                        P.inc(ins, "V")
            if not do_ln:
                continue
            if outT_src is not None:
                P.wait(VEC, "V")
                P.wait(SCA, "V")
            ins = VEC.reduce_sum(stt[:, o:o + 1], rt, axis=AX.X)
            P.inc(ins, "V")
            ins = SCA.square(sq, rt)
            P.inc(ins, "S")
            P.wait(VEC, "S")
            ins = VEC.reduce_sum(stt[:, o + 1:o + 2], sq, axis=AX.X)
            P.inc(ins, "V")
            P.wait(VEC, "V")
            VEC.tensor_scalar(stt[:, o + 2:o + 3], stt[:, o:o + 1], 1.0 / Dm, None, ALU.mult)
            ins = VEC.tensor_scalar(stt[:, o + 3:o + 4], stt[:, o + 1:o + 2], 1.0 / Dm, None, ALU.mult)
            P.inc(ins, "V")
            P.wait(VEC, "V")
            ins = VEC.tensor_mul(stt[:, o + 5:o + 6], stt[:, o + 2:o + 3], stt[:, o + 2:o + 3])
            P.inc(ins, "V")
            P.wait(VEC, "V")
            ins = VEC.tensor_sub(stt[:, o + 4:o + 5], stt[:, o + 3:o + 4], stt[:, o + 5:o + 6])
            P.inc(ins, "V")
            P.wait(SCA, "V")
            ins = SCA.activation(stt[:, o + 6:o + 7], stt[:, o + 4:o + 5], AF.Ln, bias=eps_c[:, 0:1])
            P.inc(ins, "S")
            P.wait(SCA, "S")
            ins = SCA.activation(stt[:, o + 7:o + 8], stt[:, o + 6:o + 7], AF.Exp, scale=-0.5)
            P.inc(ins, "S")
            P.wait(VEC, "S")
            ins = VEC.tensor_scalar(xnb_t[:, (j % 2) * Dm:(j % 2 + 1) * Dm], rt,
                                    stt[:, o + 2:o + 3], stt[:, o + 7:o + 8], ALU.subtract, ALU.mult)
            P.inc(ins, "V")
            if router:
                ins = VEC.tensor_scalar(BUF["xnf_t"][:, (j % 2) * Dm:(j % 2 + 1) * Dm], rt,
                                        stt[:, o + 2:o + 3], stt[:, o + 7:o + 8],
                                        ALU.subtract, ALU.mult)
                P.inc(ins, "V")
            P.wait(TEN, "V")
            for g in range(2):
                if g:
                    P.wait(TEN, "V")
                for jj in range(4):
                    c = g * 4 + jj
                    pt = bankb(b_trb + g)[:, jj * 128:(jj + 1) * 128]
                    ins = TEN.transpose(pt, xnb_t[:, (j % 2) * Dm + c * 128:(j % 2) * Dm + (c + 1) * 128],
                                        identb[:])
                P.inc(ins, "T")
                P.wait(VEC, "T")
                dst = xnT_st[:].rearrange("p (c m) -> p c m", c=8)[:, 4 * g:4 * g + 4,
                                                                  j * 128:(j + 1) * 128]
                ins = VEC.tensor_copy(dst,
                                      bankb(b_trb + g)[:, 0:512].rearrange("p (c m) -> p c m", c=4))
                P.inc(ins, "V")
            if not router:
                continue
            for g in range(2):
                P.wait(TEN, "V")
                for jj in range(4):
                    c = g * 4 + jj
                    pt = bank(b_trf + g)[:, jj * 128:(jj + 1) * 128]
                    ins = TEN.transpose(pt, BUF["xnf_t"][:, (j % 2) * Dm + c * 128:
                                                         (j % 2) * Dm + (c + 1) * 128], identf[:])
                P.inc(ins, "T")
                P.wait(VEC, "T")
                ins = VEC.tensor_copy(BUF["xnfT_st"][:, (j % 2) * Dm + g * 512:
                                                     (j % 2) * Dm + (g + 1) * 512],
                                      bank(b_trf + g))
                P.inc(ins, "V")
            P.wait(TEN, "V")
            rpb = bank(b_rt)[:, 0:16]
            for c in range(NCH):
                TEN.matmul(rpb[:, 0:12],
                           BUF["xnfT_st"][:, (j % 2) * Dm + c * 128:(j % 2) * Dm + (c + 1) * 128],
                           wr_sb[:, c * 12:(c + 1) * 12], start=(c == 0), stop=False)
            ins = TEN.matmul(rpb[:, 0:12], onesc_f[:], br_sb[:], start=False, stop=True)
            P.inc(ins, "T")
            P.wait(VEC, "T")
            ins = VEC.tensor_scalar(lgt[:, (j % 2) * 16:(j % 2) * 16 + 12], rpb[:, 0:12],
                                    1.0 / 1.5, None, ALU.mult)
            P.inc(ins, "V")
            o2 = (j % 2) * 16
            P.wait(VEC, "V")
            ins = VEC.max(m8b[:, (j % 2) * 8:(j % 2 + 1) * 8], lgt[:, o2:o2 + 16])
            P.inc(ins, "V")
            m1 = m8b[:, (j % 2) * 8 + 0:(j % 2) * 8 + 1]
            m2 = m8b[:, (j % 2) * 8 + 1:(j % 2) * 8 + 2]
            P.wait(VEC, "V")
            ins = VEC.tensor_sub(stt[:, o + 8:o + 9], m1, m2)
            P.inc(ins, "V")
            P.wait(SCA, "V")
            SCA.activation(stt[:, o + 9:o + 10], stt[:, o + 8:o + 9], AF.Sigmoid)
            ins = SCA.activation(stt[:, o + 10:o + 11], stt[:, o + 8:o + 9], AF.Sigmoid, scale=-1.0)
            P.inc(ins, "S")
            ins = VEC.tensor_scalar(gmsk[:, o2:o2 + 16], lgt[:, o2:o2 + 16], m1, 1000.0,
                                    ALU.not_equal, ALU.mult)
            P.inc(ins, "V")
            P.wait(VEC, "V")
            ins = VEC.tensor_add(gmsk[:, o2:o2 + 16], gmsk[:, o2:o2 + 16], iota16[:])
            P.inc(ins, "V")
            P.wait(VEC, "V")
            ins = VEC.tensor_reduce(stt[:, o + 11:o + 12], gmsk[:, o2:o2 + 16], axis=AX.X, op=ALU.min)
            P.inc(ins, "V")
            P.wait(VEC, "V")
            ins = VEC.tensor_scalar(gmsk[:, o2:o2 + 16], lgt[:, o2:o2 + 16], m2, 1000.0,
                                    ALU.not_equal, ALU.mult)
            P.inc(ins, "V")
            P.wait(VEC, "V")
            ins = VEC.tensor_add(gmsk[:, o2:o2 + 16], gmsk[:, o2:o2 + 16], iota16[:])
            P.inc(ins, "V")
            P.wait(VEC, "V")
            ins = VEC.tensor_scalar(gtok[:, o2:o2 + 16], iota16[:], stt[:, o + 11:o + 12], 1000.0,
                                    ALU.is_equal, ALU.mult)
            P.inc(ins, "V")
            P.wait(VEC, "V")
            ins = VEC.tensor_add(gmsk[:, o2:o2 + 16], gmsk[:, o2:o2 + 16], gtok[:, o2:o2 + 16])
            P.inc(ins, "V")
            P.wait(VEC, "V")
            ins = VEC.tensor_reduce(stt[:, o + 12:o + 13], gmsk[:, o2:o2 + 16], axis=AX.X, op=ALU.min)
            P.inc(ins, "V")
            P.wait(VEC, "V"); P.wait(VEC, "S")
            ins = VEC.tensor_scalar(gmsk[:, o2:o2 + 16], iota16[:], stt[:, o + 11:o + 12],
                                    stt[:, o + 9:o + 10], ALU.is_equal, ALU.mult)
            P.inc(ins, "V")
            ins = VEC.tensor_scalar(gtok[:, o2:o2 + 16], iota16[:], stt[:, o + 12:o + 13],
                                    stt[:, o + 10:o + 11], ALU.is_equal, ALU.mult)
            P.inc(ins, "V")
            P.wait(VEC, "V")
            ins = VEC.tensor_add(gtok[:, o2:o2 + 16], gtok[:, o2:o2 + 16], gmsk[:, o2:o2 + 16])
            P.inc(ins, "V")
            P.wait(VEC, "V")
            ins = VEC.memset(gtok[:, o2 + 12:o2 + 13], 1.0)
            P.inc(ins, "V")
            P.wait(TEN, "V")
            gpb = bank(b_trf)[:, 0:128]
            ins = TEN.transpose(gpb[0:16, :], gtok[:, o2:o2 + 16], identf[:])
            P.inc(ins, "T")
            P.wait(VEC, "T")
            ins = VEC.tensor_copy(gstg2[:, j * 128:(j + 1) * 128], gpb[0:16, :])
            P.inc(ins, "V")
            ins = VEC.tensor_copy(gstg2b[:, j * 128:(j + 1) * 128], gpb[0:16, :])
            P.inc(ins, "V")
        if R_dst is not None:
            P.wait(GPS, "V")
            ins = GPS.dma_start(out=P.dyn(GPS, R_dst.tensor, svar, svc * 524288, svk * 524288,
                                          rpat4),
                                in_=Rt[:].rearrange("p (t m) -> p t m", t=4))
            P.inc(ins, "O" if r_dst_is_out else "QG")
        if do_ln:
            P.wait(GPS, "V")
            ins = GPS.dma_start(
                out=P.dyn(GPS, xnT_dst.tensor, svar, svc * 512, svk * 512,
                          [[4096, 128], [524288, 8], [1, 512]]),
                in_=xnT_st[:].rearrange("p (c m) -> p c m", c=8))
            P.inc(ins, "QG")
        if router:
            P.wait(GPS, "V")
            ins = GPS.dma_start(out=P.dyn(GPS, gate_dst.tensor, svar, svc * 512, svk * 512,
                                          [[4096, 16], [1, 512]]),
                                in_=gstg2[:])
            P.inc(ins, "QG")
            ins = GPS.dma_start(out=P.dyn(GPS, gateb_dst.tensor, svar, svc * 512, svk * 512,
                                          [[4096, 16], [1, 512]]),
                                in_=gstg2b[:])
            P.inc(ins, "QG")

    if os.environ.get("MK_EMPTY"):
        _finish(nc, P, ctx, dbg, scr, out, x_in, upto=0 if not os.environ.get("MK_NODBG") else -1)
        return nc, ctx
    for tt_s in range(NT // 4):
        emit_boundary_strip(None, 0, tt_s, x_in, None, None, xnT1, False, True, 0, 6, 5)
    nc.all_engine_barrier()
    if phases <= 0:
        _finish(nc, P, ctx, dbg, scr, out, x_in, upto=0)
        return nc, ctx

    pctx = ExitStack()
    psb = lambda name, shape, dt: pctx.enter_context(nc.sbuf_tensor(name, shape, dt))
    xnp = psb("xnp", [128, 2 * S], BF16)
    qTa = psb("qTa", [66, S], BF16)
    kTa = psb("kTa", [66, S], BF16)
    v_sb = psb("v_sb", [128, NT * 64], BF16)
    s_row = psb("s_row", [128, S], F32)
    Pdump = psb("Pdump", [128, 1024], BF16)
    sump = psb("sump", [128, 8], F32)
    augc = psb("augc", [128, 8], F32)
    augH = psb("augH", [128, 32], BF16)
    augL = psb("augL", [128, 32], BF16)
    augsb = psb("augsb", [32, 2 * 128], BF16)
    PT_sb = psb("PT_sb", [128, 2 * 512], BF16)
    wqa_f = psb("wqa_f", [64, 3 * 64], F32)
    wqa_b = psb("wqa_b", [128, 3 * 64], BF16)
    BUF["xnf_t"] = psb("xnf_t_a", [128, 2 * Dm], F32)
    BUF["xnfT_st"] = psb("xnfT_st_a", [128, 2 * Dm], F32)

    ins = SYN.dma_start(out=wqa_f[:, 0:64], in_=wqa); P.inc(ins, "QW")
    ins = SYN.dma_start(out=wqa_f[:, 64:128], in_=wka); P.inc(ins, "QW")
    ins = SYN.dma_start(out=wqa_f[:, 128:192], in_=wva); P.inc(ins, "QW")
    P.wait(VEC, "QW")
    VEC.tensor_copy(wqa_b[0:64, :], wqa_f[:])
    ins = VEC.tensor_copy(wqa_b[64:128, :], wqa_f[:]); P.inc(ins, "V")
    ins = VEC.memset(kTa[64:66, :], 1.0); P.inc(ins, "V")
    avh_st = psb("avh_st", [128, S], BF16)
    avs_sb = psb("avs_sb", [128, 8 * 512], BF16)
    P.wait(GPS, "QG")

    with P.loop("ata", 4) as tt4:
      for par in range(2):
        pp = tt4 * 2 + par
        P.wait(GPS, "T")
        ins = GPS.dma_start(
            out=xnp[:, par * S:(par + 1) * S],
            in_=P.dyn(GPS, xnT1.tensor, tt4, 2 * 524288, par * 524288,
                      [[4096, 128], [1, 4096]]))
        P.inc(ins, "XP")
        for half in range(2):
            bp = half * 64
            P.wait(TEN, "XP")
            P.wait(TEN, "V")
            for st8 in range(8):
                bq = bank(st8 % 2)
                P.wait(TEN, "S")
                ins = TEN.matmul(bq[0:64, 0:512], wqa_b[bp:bp + 64, 0:64],
                                 xnp[bp:bp + 64, par * S + st8 * 512: par * S + (st8 + 1) * 512],
                                 start=True, stop=True)
                P.inc(ins, "T")
                P.wait(SCA, "T")
                ins = SCA.activation(qTa[0:64, st8 * 512:(st8 + 1) * 512], bq[0:64, 0:512], AF.Copy)
                P.inc(ins, "S")
            for st8 in range(8):
                bq = bank(2 + st8 % 2)
                P.wait(TEN, "S")
                ins = TEN.matmul(bq[0:64, 0:512], wqa_b[bp:bp + 64, 64:128],
                                 xnp[bp:bp + 64, par * S + st8 * 512: par * S + (st8 + 1) * 512],
                                 start=True, stop=True)
                P.inc(ins, "T")
                P.wait(SCA, "T")
                ins = SCA.activation(kTa[0:64, st8 * 512:(st8 + 1) * 512], bq[0:64, 0:512], AF.Copy)
                P.inc(ins, "S")
            for tt8 in range(NT):
                bq = bank(4 + tt8 % 2)
                P.wait(TEN, "V")
                ins = TEN.matmul(bq[0:128, 0:64],
                                 xnp[bp:bp + 64, par * S + tt8 * 128: par * S + (tt8 + 1) * 128],
                                 wqa_b[bp:bp + 64, 128:192], start=True, stop=True)
                P.inc(ins, "T")
                P.wait(VEC, "T")
                ins = VEC.tensor_copy(v_sb[:, tt8 * 64:(tt8 + 1) * 64], bq[0:128, 0:64])
                P.inc(ins, "V")
            for qi in range(NT):
                kvalid = (qi + 1) * 128
                nks = (kvalid + 511) // 512
                P.wait(TEN, "S")
                for ks in range(nks):
                    N = min(512, kvalid - ks * 512)
                    bq = bank(ks % 2)
                    if ks >= 2:
                        P.wait(TEN, "S")
                    ins = TEN.matmul(bq[0:128, 0:N], qTa[0:64, qi * 128:(qi + 1) * 128],
                                     kTa[0:64, ks * 512:ks * 512 + N], start=True, stop=True)
                    P.inc(ins, "T")
                    P.wait(SCA, "T")
                    ins = SCA.activation(s_row[:, ks * 512:ks * 512 + N], bq[0:128, 0:N],
                                         AF.Copy, scale=0.125)
                    P.inc(ins, "S")
                P.wait(VEC, "S")
                ins = VEC.tensor_add(s_row[:, qi * 128:kvalid], s_row[:, qi * 128:kvalid], maskA[:])
                P.inc(ins, "V")
                P.wait(VEC, "V")
                ins = VEC.reduce_max(augc[:, 0:1], s_row[:, 0:kvalid], axis=AX.X, negate=True)
                P.inc(ins, "V")
                P.wait(SCA, "V")
                nch_ = (kvalid + 1023) // 1024
                for j in range(nch_):
                    w = min(1024, kvalid - j * 1024)
                    if j:
                        P.wait(SCA, "S")
                    ins = SCA.activation(Pdump[:, 0:w], s_row[:, j * 1024:j * 1024 + w], AF.Exp,
                                         bias=augc[:, 0:1], accum_out=sump[:, j:j + 1])
                    P.inc(ins, "S")
                P.wait(VEC, "S")
                ins = VEC.reduce_sum(augc[:, 1:2], sump[:, 0:nch_], axis=AX.X)
                P.inc(ins, "V")
                P.wait(SCA, "V")
                ins = SCA.activation(augc[:, 2:3], augc[:, 1:2], AF.Ln)
                P.inc(ins, "S")
                P.wait(VEC, "S")
                ins = VEC.tensor_sub(augc[:, 3:4], augc[:, 0:1], augc[:, 2:3])
                P.inc(ins, "V")
                P.wait(VEC, "V")
                ins = VEC.tensor_scalar(augc[:, 4:5], augc[:, 3:4], 8.0, None, ALU.mult)
                P.inc(ins, "V")
                P.wait(VEC, "V")
                ins = VEC.tensor_copy(augH[:, qi:qi + 1], augc[:, 4:5])
                P.inc(ins, "V")
                P.wait(VEC, "V")
                ins = VEC.tensor_sub(augL[:, qi:qi + 1], augc[:, 4:5], augH[:, qi:qi + 1])
                P.inc(ins, "V")
            P.wait(TEN, "V")
            ins = TEN.transpose(bankb(0)[0:32, 0:128], augH[:], identb[:])
            P.inc(ins, "T")
            ins = TEN.transpose(bankb(1)[0:32, 0:128], augL[:], identb[:])
            P.inc(ins, "T")
            P.wait(VEC, "T")
            ins = VEC.tensor_copy(augsb[:, 0:128], bankb(0)[0:32, 0:128])
            P.inc(ins, "V")
            ins = VEC.tensor_copy(augsb[:, 128:256], bankb(1)[0:32, 0:128])
            P.inc(ins, "V")
            P.wait(GPS, "V")
            ins = GPS.dma_start(out=augrow_d[0], in_=augsb[:, 0:128])
            P.inc(ins, "QG")
            ins = GPS.dma_start(out=augrow_d[1], in_=augsb[:, 128:256])
            P.inc(ins, "QG")
            P.wait(GPS, "QG")
            ins = GPS.dma_start(out=qTa[64:65, :], in_=augrow_d[0].rearrange("p m -> (p m)").unsqueeze(0))
            P.inc(ins, "QG")
            ins = GPS.dma_start(out=qTa[65:66, :], in_=augrow_d[1].rearrange("p m -> (p m)").unsqueeze(0))
            P.inc(ins, "QG")
            P.wait(TEN, "QG")
            for qs in range(8):
                nkj = 4 * (qs + 1)
                for kj in range(nkj):
                    bq = bank(kj % 2)
                    P.wait(TEN, "S")
                    if kj >= 4 * qs:
                        P.wait(TEN, "V")
                    ins = TEN.matmul(bq[0:128, 0:512], kTa[0:66, kj * 128:(kj + 1) * 128],
                                     qTa[0:66, qs * 512:(qs + 1) * 512], start=True, stop=True,
                                     skip_group_check=True)
                    P.inc(ins, "T")
                    P.wait(SCA, "T")
                    psl = PT_sb[:, (kj % 2) * 512:(kj % 2) * 512 + 512]
                    ins = SCA.activation(psl, bq[0:128, 0:512], AF.Exp, scale=0.125)
                    P.inc(ins, "S")
                    jloc = kj - 4 * qs
                    if jloc >= 0:
                        P.wait(VEC, "S")
                        if jloc > 0:
                            ins = VEC.memset(psl[:, 0:jloc * 128], 0.0)
                            P.inc(ins, "V")
                        ins = VEC.tensor_mul(psl[:, jloc * 128:(jloc + 1) * 128],
                                             psl[:, jloc * 128:(jloc + 1) * 128], maskMb[:])
                        P.inc(ins, "V")
                        P.wait(TEN, "V")
                    else:
                        P.wait(TEN, "S")
                    ins = TEN.matmul(bank(4)[0:64, 0:512], v_sb[:, kj * 64:(kj + 1) * 64], psl,
                                     start=(kj == 0), stop=(kj == nkj - 1), skip_group_check=True)
                    P.inc(ins, "T")
                P.wait(VEC, "T")
                P.wait(VEC, "QG")
                ins = VEC.tensor_copy(avh_st[bp:bp + 64, qs * 512:(qs + 1) * 512],
                                      bank(4)[0:64, 0:512])
                P.inc(ins, "V")
                P.wait(TEN, "V")
        P.wait(GPS, "V")
        ins = GPS.dma_start(out=P.dyn(GPS, avT_d.tensor, tt4, 2 * 524288, par * 524288,
                                      [[4096, 128], [1, 4096]]),
                            in_=avh_st[:])
        P.inc(ins, "QG")
    outT_strip = s_row
    P.wait(SYN, "V")
    for s8 in range(NSTRIP * (TS // 512)):
        P.wait(GPS, "T")
        P.wait(GPS, "QG")
        ins = GPS.dma_start(out=avs_sb[:].rearrange("p (k m) -> p k m", k=8),
                            in_=bass.AP(avT_d.tensor, s8 * 512,
                                        [[4096, 128], [524288, 8], [1, 512]]))
        P.inc(ins, "AV")
        for oc in range(8):
            P.wait(SYN, "P")
            ins = SYN.dma_start(out=wstA[:, (oc % 2) * 2048:(oc % 2) * 2048 + 1024]
                                .rearrange("p (k m) -> p k m", k=8),
                                in_=woa[oc].rearrange("p k m -> p k m"))
            P.inc(ins, "WA" if oc % 2 == 0 else "WB")
            P.wait(GPS, "WA" if oc % 2 == 0 else "WB")
            P.wait(GPS, "T")
            ins = GPS.tensor_copy(wbufA[:, (oc % 2) * 1024:(oc % 2) * 1024 + 1024],
                                  wstA[:, (oc % 2) * 2048:(oc % 2) * 2048 + 1024])
            P.inc(ins, "P")
            P.wait(TEN, "P")
            P.wait(TEN, "V")
            bq = bank(oc % 2)
            for c in range(NCH):
                if c == 0:
                    P.wait(TEN, "AV")
                ins = TEN.matmul(bq[0:128, 0:512],
                                 wbufA[:, (oc % 2) * 1024 + c * 128:(oc % 2) * 1024 + (c + 1) * 128],
                                 avs_sb[:, c * 512:(c + 1) * 512], start=(c == 0), stop=(c == 7))
            P.inc(ins, "T")
            P.wait(VEC, "T")
            ins = VEC.tensor_copy(outT_strip[:, oc * 512:(oc + 1) * 512], bq[0:128, 0:512])
            P.inc(ins, "V")
        emit_boundary_strip(None, 0, s8, x_in,
                            lambda c, j: outT_strip[:, c * 512 + j * 128: c * 512 + (j + 1) * 128],
                            R1, xnT2, True, True, 2, 2, 5, gate_dst=wg_d, gateb_dst=wgb_d)
    nc.all_engine_barrier()
    pctx.close()
    if phases <= 1:
        _finish(nc, P, ctx, dbg, scr, out, x_in, upto=1)
        return nc, ctx

    def emit_ffn(inv, xnT_src, gate_src, R_src, R_dst, xnT_dst, final):
        STG = GPS if inv == 1 else VEC
        STGK = "P" if inv == 1 else "V"
        YST = GPS if inv == 1 else SCA
        YSTK = "QG" if inv == 1 else "QV"
        fctx = ExitStack()
        fsb = lambda name, shape, dt: fctx.enter_context(nc.sbuf_tensor(name + str(inv), shape, dt))
        xnS = fsb("xnS", [128, 8 * TS], BF16)
        act_sb = fsb("act_sb", [128, 24 * TS], BF16)
        yr = fsb("yr", [128, 8 * TS], F32)
        wstY = fsb("wstY", [128, 1536], F32)
        wbufY = fsb("wbufY", [128, 2 * 3072], BF16)
        wbc_sb = fsb("wbc_sb", [128, TS], F32)
        tmpa = fsb("tmpa", [128, 2 * TS], BF16)
        tmpg = fsb("tmpg", [128, 2 * TS], BF16)
        be1ag_sb = fsb("be1ag_sb", [128, 48], F32)
        bs1c = fsb("bs1c", [128, 32], F32)
        wgb13 = fsb("wgb13", [16, 512], BF16)
        be2b = fsb("be2b", [16, Dm], BF16)
        bias_cur = fsb("bias_cur", [128, 2 * 2], F32)
        grow_f = fsb("grow_f", [1, TS], F32)
        grow_b = fsb("grow_b", [1, TS], BF16)

        ins = SYN.dma_start(out=bs1c[:, 0:16], in_=bs1a); P.inc(ins, "QW")
        ins = SYN.dma_start(out=bs1c[:, 16:32], in_=bs1g); P.inc(ins, "QW")
        P.wait(SYN, "P"); P.wait(SYN, "T")
        ins = SYN.dma_start(out=wstA[0:13, 0:Dm], in_=be2all)
        P.inc(ins, "QW")
        P.wait(VEC, "QW")
        ins = VEC.tensor_copy(be2b[0:13, :], wstA[0:13, 0:Dm])
        P.inc(ins, "V")
        P.wait(SYN, "V")

        def act_stage(nf, lhs_dyn, bias_a, bias_g, gated):
            pass

        with P.loop(f"ffnS{inv}", NSTRIP) as sv:
            P.wait(GPS, "T")
            P.wait(GPS, "QG")
            ins = GPS.dma_start(out=xnS[:].rearrange("p (c m) -> p c m", c=8),
                                in_=P.dyn(GPS, xnT_src.tensor, sv, TS, 0,
                                          [[4096, 128], [524288, 8], [1, TS]]))
            P.inc(ins, "XN")
            P.wait(VEC, "T")
            P.wait(VEC, "V")
            P.wait(VEC, YSTK)
            ins = VEC.memset(yr[:], 0.0)
            P.inc(ins, "V")



            def emit_mlp(nf, we_dyn_a, bias_a_ap, bias_g_ap, gated, ksrc_n, aug_lhs, aug_rhs,
                         y_dyn, loopname):
                pass

            NEX = NE
            with P.loop(f"ffnE{inv}", NEX) as ev:
                P.wait(SCA, "T")
                P.wait(SCA, "V")
                ins = SCA.dma_start(out=grow_f[:],
                                    in_=P.dyn2(SCA, gate_src.tensor, ev, S, sv, TS, 0,
                                               [[1, 1], [1, TS]]))
                P.inc(ins, "GR")
                ins = SCA.dma_start(out=be1ag_sb[:],
                                    in_=P.dyn(SCA, be1ag.tensor, ev, 128 * 48, 0,
                                              [[48, 128], [1, 48]]))
                P.inc(ins, "BE")

                P.wait(SCA, "GR"); P.wait(SCA, "BE")
                ins = SCA.nop()
                P.inc(ins, "S")
                P.wait(VEC, "S")
                P.wait(VEC, "T")
                ins = VEC.tensor_copy(grow_b[:], grow_f[:])
                P.inc(ins, "V")
                P.wait(TEN, "S"); P.wait(TEN, "V")
                TEN.matmul(bank(4)[:, 0:512], onesc_f[:], grow_f[0:1, 0:512], start=True, stop=True)
                ins = TEN.matmul(bank(5)[:, 0:512], onesc_f[:], grow_f[0:1, 512:1024],
                                 start=True, stop=True)
                P.inc(ins, "T")
                P.wait(VEC, "T")
                VEC.tensor_copy(wbc_sb[:, 0:512], bank(4)[:, 0:512])
                ins = VEC.tensor_copy(wbc_sb[:, 512:1024], bank(5)[:, 0:512])
                P.inc(ins, "V")

                with P.loop(f"ffnF{inv}", 12) as fv:
                    for sl in range(2):
                        P.wait(SYN, "P")
                        ins = SYN.dma_start(
                            out=wstA[:, sl * 2048:(sl + 1) * 2048].rearrange(
                                "p (h m) -> p h m", h=2),
                            in_=P.dyn2(SYN, we1.tensor, ev, 48 * 131072, fv, 2 * 131072,
                                       sl * 131072,
                                       [[1024, 128], [24 * 131072, 2], [1, 1024]]))
                        P.inc(ins, "WA" if sl == 0 else "WB")
                        P.wait(GPS, "WA" if sl == 0 else "WB")
                        P.wait(GPS, "T")
                        GPS.tensor_copy(wbufA[:, sl * 1024:(sl + 1) * 1024],
                                        wstA[:, sl * 2048:sl * 2048 + 1024])
                        ins = GPS.tensor_copy(wbufG[:, sl * 1024:(sl + 1) * 1024],
                                              wstA[:, sl * 2048 + 1024:sl * 2048 + 2048])
                        P.inc(ins, "P")
                        P.wait(TEN, "P"); P.wait(TEN, "V"); P.wait(TEN, "S")
                        P.wait(TEN, "XN")
                        pa = bank(0 + 2 * sl, 2)
                        pg = bank(4 + 2 * sl, 2)
                        for st2 in range(2):
                            for k in range(8):
                                ins = TEN.matmul(
                                    pa[:, st2 * 512:(st2 + 1) * 512],
                                    wbufA[:, sl * 1024 + k * 128: sl * 1024 + (k + 1) * 128],
                                    xnS[:, k * TS + st2 * 512: k * TS + st2 * 512 + 512],
                                    start=(k == 0), stop=(k == 7), skip_group_check=True)
                        P.inc(ins, "T")
                        for st2 in range(2):
                            for k in range(8):
                                ins = TEN.matmul(
                                    pg[:, st2 * 512:(st2 + 1) * 512],
                                    wbufG[:, sl * 1024 + k * 128: sl * 1024 + (k + 1) * 128],
                                    xnS[:, k * TS + st2 * 512: k * TS + st2 * 512 + 512],
                                    start=(k == 0), stop=(k == 7), skip_group_check=True)
                        P.inc(ins, "T")
                        P.wait(STG, "S"); P.wait(STG, "V")
                        bsl = be1ag_sb[:, sl * 24:(sl + 1) * 24][:, ts(fv, 2)]
                        ins = STG.tensor_copy(bias_cur[:, sl * 2:(sl + 1) * 2], bsl)
                        P.inc(ins, STGK)
                        P.wait(VEC, "T")
                        P.wait(VEC, STGK)
                        ins = VEC.scalar_tensor_tensor(tmpa[:, sl * TS:(sl + 1) * TS], pa,
                                                       bias_cur[:, sl * 2:sl * 2 + 1],
                                                       wbc_sb[:], ALU.add, ALU.mult)
                        P.inc(ins, "V")
                        P.wait(SCA, "T")
                        P.wait(SCA, STGK)
                        ins = SCA.activation(tmpg[:, sl * TS:(sl + 1) * TS], pg, AF.Gelu,
                                             bias=bias_cur[:, sl * 2 + 1:sl * 2 + 2])
                        P.inc(ins, "S")
                        P.wait(VEC, "S"); P.wait(VEC, "V")
                        P.wait(VEC, "T")
                        ins = VEC.tensor_mul(act_sb[:, sl * 12 * TS:(sl + 1) * 12 * TS][:, ts(fv, TS)],
                                             tmpa[:, sl * TS:(sl + 1) * TS],
                                             tmpg[:, sl * TS:(sl + 1) * TS])
                        P.inc(ins, "V")
                with P.loop(f"ffnY{inv}", 4) as ov:
                    for po in range(2):
                        for h in range(2):
                            P.wait(SYN, "P")
                            ins = SYN.dma_start(
                                out=wstY[:, 0:1536],
                                in_=P.dyn2(SYN, we2.tensor, ev, 8 * 393216, ov, 2 * 393216,
                                           po * 393216 + h * 1536,
                                           [[3072, 128], [1, 1536]]))
                            P.inc(ins, "WY")
                            P.wait(GPS, "WY")
                            P.wait(GPS, "T")
                            ins = GPS.tensor_copy(wbufY[:, po * 3072 + h * 1536:
                                                        po * 3072 + (h + 1) * 1536],
                                                  wstY[:, 0:1536])
                            P.inc(ins, "P")
                        py = bank(0 + 2 * po, 2)
                        P.wait(TEN, "P"); P.wait(TEN, "V")
                        P.wait(TEN, "S")
                        for st2 in range(2):
                            for k in range(24):
                                ins = TEN.matmul(py[:, st2 * 512:(st2 + 1) * 512],
                                           wbufY[:, po * 3072 + k * 128: po * 3072 + (k + 1) * 128],
                                           act_sb[:, k * TS + st2 * 512: k * TS + st2 * 512 + 512],
                                           start=(k == 0),
                                           stop=(k == 23), skip_group_check=True)
                        P.inc(ins, "T")
                        P.wait(VEC, "T")
                        ydst = yr[:, po * 4 * TS:(po + 1) * 4 * TS][:, ts(ov, TS)]
                        ins = VEC.tensor_add(ydst, ydst, py[:, 0:1024])
                        P.inc(ins, "V")
            for fc in range(16):
                sl = fc % 2
                P.wait(SYN, "P")
                ins = SYN.dma_start(
                    out=wstA[:, sl * 2048:(sl + 1) * 2048].rearrange("p (h m) -> p h m", h=2),
                    in_=bass.AP(ws1.tensor, fc * 131072,
                                [[1024, 128], [16 * 131072, 2], [1, 1024]]))
                P.inc(ins, "WA" if sl == 0 else "WB")
                P.wait(GPS, "WA" if sl == 0 else "WB")
                P.wait(GPS, "T")
                GPS.tensor_copy(wbufA[:, sl * 1024:(sl + 1) * 1024],
                                wstA[:, sl * 2048:sl * 2048 + 1024])
                ins = GPS.tensor_copy(wbufG[:, sl * 1024:(sl + 1) * 1024],
                                      wstA[:, sl * 2048 + 1024:sl * 2048 + 2048])
                P.inc(ins, "P")
                P.wait(TEN, "P"); P.wait(TEN, "V"); P.wait(TEN, "S")
                P.wait(TEN, "XN")
                pa = bank(0 + 2 * sl, 2)
                pg = bank(4 + 2 * sl, 2)
                for st2 in range(2):
                    for k in range(8):
                        ins = TEN.matmul(pa[:, st2 * 512:(st2 + 1) * 512],
                                         wbufA[:, sl * 1024 + k * 128: sl * 1024 + (k + 1) * 128],
                                         xnS[:, k * TS + st2 * 512: k * TS + st2 * 512 + 512],
                                         start=(k == 0), stop=(k == 7), skip_group_check=True)
                P.inc(ins, "T")
                for st2 in range(2):
                    for k in range(8):
                        ins = TEN.matmul(pg[:, st2 * 512:(st2 + 1) * 512],
                                         wbufG[:, sl * 1024 + k * 128: sl * 1024 + (k + 1) * 128],
                                         xnS[:, k * TS + st2 * 512: k * TS + st2 * 512 + 512],
                                         start=(k == 0), stop=(k == 7), skip_group_check=True)
                P.inc(ins, "T")
                P.wait(VEC, "T")
                ins = VEC.tensor_scalar(tmpa[:, sl * TS:(sl + 1) * TS], pa,
                                        bs1c[:, fc:fc + 1], None, ALU.add)
                P.inc(ins, "V")
                P.wait(SCA, "T")
                ins = SCA.activation(tmpg[:, sl * TS:(sl + 1) * TS], pg, AF.Gelu,
                                     bias=bs1c[:, 16 + fc:17 + fc])
                P.inc(ins, "S")
                P.wait(VEC, "S"); P.wait(VEC, "V")
                P.wait(VEC, "T")
                ins = VEC.tensor_mul(act_sb[:, fc * TS:(fc + 1) * TS],
                                     tmpa[:, sl * TS:(sl + 1) * TS],
                                     tmpg[:, sl * TS:(sl + 1) * TS])
                P.inc(ins, "V")
            for oc in range(8):
                po = oc % 2
                for h in range(2):
                    P.wait(GPS, "P")
                    ins = GPS.dma_start(out=wstY[:, 0:1024],
                                        in_=bass.AP(ws2.tensor, oc * 262144 + h * 1024,
                                                    [[2048, 128], [1, 1024]]))
                    P.inc(ins, "QG")
                    P.wait(GPS, "QG")
                    P.wait(GPS, "T")
                    ins = GPS.tensor_copy(wbufY[:, po * 3072 + h * 1024: po * 3072 + (h + 1) * 1024],
                                          wstY[:, 0:1024])
                    P.inc(ins, "P")
                py = bank(0 + 2 * po, 2)
                P.wait(TEN, "P"); P.wait(TEN, "V")
                for st2 in range(2):
                    for k in range(16):
                        ins = TEN.matmul(py[:, st2 * 512:(st2 + 1) * 512],
                                   wbufY[:, po * 3072 + k * 128: po * 3072 + (k + 1) * 128],
                                   act_sb[:, k * TS + st2 * 512: k * TS + st2 * 512 + 512],
                                   start=(k == 0), stop=(k == 15), skip_group_check=True)
                P.inc(ins, "T")
                P.wait(VEC, "T")
                ypos = (oc % 2) * 4 + oc // 2
                ins = VEC.tensor_add(yr[:, ypos * TS:(ypos + 1) * TS],
                                     yr[:, ypos * TS:(ypos + 1) * TS], py[:, 0:1024])
                P.inc(ins, "V")

            P.wait(YST, "V"); P.wait(YST, "P")
            ins = YST.dma_start(out=P.dyn(YST, yr_d.tensor, sv, 128 * 8 * TS, 0,
                                          [[8 * TS, 128], [1, 8 * TS]]),
                                in_=yr[:])
            P.inc(ins, YSTK)
        gateb_src = wgb_d if inv == 1 else wgb_d2
        for s8 in range(8):
            P.wait(GPS, "T")
            P.wait(GPS, YSTK)
            P.wait(SYN, "T")
            ins = SYN.dma_start(out=wgb13[:], in_=bass.AP(gateb_src.tensor, s8 * 512,
                                                          [[4096, 16], [1, 512]]))
            P.inc(ins, "WY")
            P.wait(TEN, "WY")
            base_ = (s8 // 2) * 128 * 8 * TS + (s8 % 2) * 512
            ins = GPS.dma_start(
                out=yr[:, 0:4096].rearrange("p (c m) -> p c m", c=8)[:, 0:8:2, :],
                in_=bass.AP(yr_d.tensor, base_, [[8 * TS, 128], [TS, 4], [1, 512]]))
            P.inc(ins, "AV")
            ins = GPS.dma_start(
                out=yr[:, 0:4096].rearrange("p (c m) -> p c m", c=8)[:, 1:8:2, :],
                in_=bass.AP(yr_d.tensor, base_ + 4 * TS, [[8 * TS, 128], [TS, 4], [1, 512]]))
            P.inc(ins, "AV")
            P.wait(TEN, "AV")
            emit_boundary_strip(
                None, 0, s8, R_src,
                lambda c, j: yr[:, c * 512 + j * 128: c * 512 + (j + 1) * 128],
                R_dst, xnT_dst, False, not final, 0, 2, 5,
                r_dst_is_out=final, gbias=(wgb13, be2b))
        nc.all_engine_barrier()
        fctx.close()

    emit_ffn(1, xnT2, wg_d, R1, R2, xnT3, False)
    if phases <= 2:
        _finish(nc, P, ctx, dbg, scr, out, x_in, upto=2)
        return nc, ctx

    bctx = ExitStack()
    bsb = lambda name, shape, dt: bctx.enter_context(nc.sbuf_tensor(name, shape, dt))
    xnB = bsb("xnB", [128, NCH * S], BF16)
    gkd_sb = bsb("gkd_sb", [128, NHEAD], F32)
    gcc_sb = bsb("gcc_sb", [64, NHEAD], F32)
    rT_st = bsb("rT_st", [128, S], BF16)
    rctx = ExitStack()
    rsb = lambda name, shape, dt: rctx.enter_context(nc.sbuf_tensor(name, shape, dt))
    ropecb = rsb("ropecb", [64, S], BF16)
    ropesb = rsb("ropesb", [64, S], BF16)
    dttb = rsb("dttb", [128, NHEAD * 128], BF16)
    gupb = rsb("gupb", [64, NHEAD * 128], BF16)
    qTb = rsb("qTb", [64, S], BF16)
    kTb = rsb("kTb", [64, S], BF16)
    vTb = rsb("vTb", [64, S], BF16)
    qsf = rsb("qsf", [64, 512], F32)
    qsw = rsb("qsw", [64, 512], F32)
    qt1 = rsb("qt1", [64, 512], F32)
    v_tok = rsb("v_tok", [128, 2 * 64], BF16)
    ktil = rsb("ktil", [128, 2 * 64], BF16)
    qtl = rsb("qtl", [64, 2 * 128], BF16)
    PT_r = rsb("PT_r", [128, 2 * 128], BF16)
    S_sb = rsb("S_sb", [64, 2 * 64], BF16)
    wbufV = rsb("wbufV", [128, 1024], BF16)
    dcur = rsb("dcur", [128, 2 * 128], BF16)
    gcur = rsb("gcur", [64, 2 * 128], BF16)
    kdcur = rsb("kdcur", [128, 2], F32)
    ccur = rsb("ccur", [64, 2], F32)

    P.wait(GPS, "QG")
    ins = GPS.dma_start(out=xnB[:].rearrange("p (c m) -> p c m", c=8),
                        in_=xnT3.rearrange("c p m -> p c m"))
    P.inc(ins, "QG")
    P.wait(SYN, "P"); P.wait(SYN, "T")
    ins = SYN.dma_start(out=wstA[0:64, 0:4096], in_=ropec_in)
    P.inc(ins, "QW")
    ins = SYN.dma_start(out=wstA[64:128, 0:4096], in_=ropes_in)
    P.inc(ins, "QW")
    P.wait(VEC, "QW")
    VEC.tensor_copy(ropecb[:], wstA[0:64, 0:4096])
    ins = VEC.tensor_copy(ropesb[:], wstA[64:128, 0:4096])
    P.inc(ins, "V")
    P.wait(SYN, "V")
    ins = SYN.dma_start(out=wstA[:, 0:2048], in_=dtt_in)
    P.inc(ins, "QW")
    ins = SYN.dma_start(out=wstA[0:64, 2048:4096], in_=gup_in)
    P.inc(ins, "QW")
    ins = SYN.dma_start(out=gkd_sb[:], in_=gkd_in)
    P.inc(ins, "QW")
    ins = SYN.dma_start(out=gcc_sb[:], in_=gcc_in)
    P.inc(ins, "QW")
    P.wait(VEC, "QW")
    VEC.tensor_copy(dttb[:], wstA[:, 0:2048])
    ins = VEC.tensor_copy(gupb[:], wstA[0:64, 2048:4096])
    P.inc(ins, "V")

    with P.loop("atb", 4) as bb4:
      for par in range(2):
        pp = bb4 * 2 + par
        hexp = pp * 2
        P.wait(SYN, "V")
        for wsrc, key, dst in [(wqb, "WA", wbufA), (wkb, "WB", wbufG), (wvb, "WY", wbufV)]:
            P.wait(SYN, "P")
            ins = SYN.dma_start(out=wstA[:, 0:1024],
                                in_=P.dyn(SYN, wsrc.tensor, bb4, 2 * 131072, par * 131072,
                                          [[1024, 128], [1, 1024]]))
            P.inc(ins, key)
            P.wait(GPS, key)
            P.wait(GPS, "T")
            ins = GPS.tensor_copy(dst[:, 0:1024], wstA[:, 0:1024])
            P.inc(ins, "P")
        P.wait(SCA, "V"); P.wait(SCA, "QW")
        SCA.copy(dcur[:], dttb[:, ts(pp, 256)])
        SCA.copy(gcur[:], gupb[:, ts(pp, 256)])
        SCA.copy(kdcur[:], gkd_sb[:, ts(pp, 2)])
        ins = SCA.copy(ccur[:], gcc_sb[:, ts(pp, 2)])
        P.inc(ins, "S")
        for half in range(2):
            bp = half * 64
            P.wait(TEN, "QG")
            for which, wbuf_, dstT, do_rope in [(0, wbufA, qTb, True), (1, wbufG, kTb, True),
                                                (2, wbufV, vTb, False)]:
                for st8 in range(8):
                    bq = bank(st8 % 2)
                    P.wait(TEN, "P"); P.wait(TEN, "V"); P.wait(TEN, "S")
                    for kc in range(8):
                        ins = TEN.matmul(bq[0:64, 0:512],
                                         wbuf_[:, kc * 128 + bp: kc * 128 + bp + 64],
                                         xnB[:, kc * S + st8 * 512: kc * S + st8 * 512 + 512],
                                         start=(kc == 0), stop=(kc == 7), skip_group_check=True)
                    P.inc(ins, "T")
                    if not do_rope:
                        P.wait(SCA, "T")
                        ins = SCA.activation(dstT[0:64, st8 * 512:(st8 + 1) * 512],
                                             bq[0:64, 0:512], AF.Copy)
                        P.inc(ins, "S")
                    else:
                        P.wait(VEC, "T")
                        P.wait(VEC, "V")
                        ins = VEC.tensor_copy(qsf[:], bq[0:64, 0:512])
                        P.inc(ins, "V")
                        P.wait(VEC, "V")
                        VEC.tensor_copy(qsw[0:32, :], qsf[32:64, :])
                        ins = VEC.tensor_copy(qsw[32:64, :], qsf[0:32, :])
                        P.inc(ins, "V")
                        P.wait(VEC, "V")
                        ins = VEC.tensor_mul(qt1[:], qsf[:], ropecb[:, st8 * 512:(st8 + 1) * 512])
                        P.inc(ins, "V")
                        P.wait(VEC, "V")
                        ins = VEC.tensor_mul(qsf[:], qsw[:], ropesb[:, st8 * 512:(st8 + 1) * 512])
                        P.inc(ins, "V")
                        P.wait(VEC, "V")
                        ins = VEC.tensor_add(dstT[0:64, st8 * 512:(st8 + 1) * 512], qt1[:], qsf[:])
                        P.inc(ins, "V")
            P.wait(VEC, "QG")
            P.wait(VEC, "S")
            ins = VEC.memset(S_sb[:], 0.0)
            P.inc(ins, "V")
            for kj in range(NT):
                sl = kj % 2
                P.wait(TEN, "S"); P.wait(TEN, "V")
                ins = TEN.matmul(bank(sl)[0:128, 0:128], kTb[:, kj * 128:(kj + 1) * 128],
                                 qTb[:, kj * 128:(kj + 1) * 128], start=True, stop=True,
                                 skip_group_check=True)
                P.inc(ins, "T")
                P.wait(VEC, "T")
                ins = VEC.tensor_mul(PT_r[:, sl * 128:(sl + 1) * 128], bank(sl)[0:128, 0:128],
                                     dcur[:, half * 128:(half + 1) * 128])
                P.inc(ins, "V")
                ins = TEN.transpose(bankb(4)[0:128, 0:64], vTb[:, kj * 128:(kj + 1) * 128],
                                    identb[0:64, 0:64])
                P.inc(ins, "T")
                ins = TEN.transpose(bankb(5)[0:128, 0:64], kTb[:, kj * 128:(kj + 1) * 128],
                                    identb[0:64, 0:64])
                P.inc(ins, "T")
                P.wait(VEC, "T")
                ins = VEC.tensor_copy(v_tok[:, sl * 64:(sl + 1) * 64], bankb(4)[0:128, 0:64])
                P.inc(ins, "V")
                ins = VEC.tensor_scalar(ktil[:, sl * 64:(sl + 1) * 64], bankb(5)[0:128, 0:64],
                                        kdcur[:, half:half + 1], None, ALU.mult)
                P.inc(ins, "V")
                ins = VEC.tensor_mul(qtl[:, sl * 128:(sl + 1) * 128],
                                     qTb[:, kj * 128:(kj + 1) * 128],
                                     gcur[:, half * 128:(half + 1) * 128])
                P.inc(ins, "V")
                P.wait(TEN, "V")
                TEN.matmul(bank(2)[0:64, 0:128], v_tok[:, sl * 64:(sl + 1) * 64],
                           PT_r[:, sl * 128:(sl + 1) * 128], start=True, stop=False,
                           skip_group_check=True)
                ins = TEN.matmul(bank(2)[0:64, 0:128], S_sb[:, (kj % 2) * 64:(kj % 2) * 64 + 64]
                                 if False else S_sb[:, (1 - sl) * 64:(2 - sl) * 64],
                                 qtl[:, sl * 128:(sl + 1) * 128], start=False, stop=True,
                                 skip_group_check=True)
                P.inc(ins, "T")
                ins = TEN.matmul(bank(3)[0:64, 0:64], ktil[:, sl * 64:(sl + 1) * 64],
                                 v_tok[:, sl * 64:(sl + 1) * 64], start=True, stop=True,
                                 skip_group_check=True)
                P.inc(ins, "T")
                P.wait(VEC, "T")
                ins = VEC.tensor_copy(rT_st[bp:bp + 64, kj * 128:(kj + 1) * 128],
                                      bank(2)[0:64, 0:128])
                P.inc(ins, "V")
                ins = VEC.scalar_tensor_tensor(S_sb[:, sl * 64:(sl + 1) * 64],
                                               S_sb[:, (1 - sl) * 64:(2 - sl) * 64],
                                               ccur[:, half:half + 1],
                                               bank(3)[0:64, 0:64], ALU.mult, ALU.add)
                P.inc(ins, "V")
        P.wait(GPS, "V")
        ins = GPS.dma_start(out=P.dyn(GPS, retT_d.tensor, bb4, 2 * 524288, par * 524288,
                                      [[4096, 128], [1, 4096]]),
                            in_=rT_st[:])
        P.inc(ins, "QG")
    nc.all_engine_barrier()
    rctx.close()
    BUF["xnf_t"] = bsb("xnf_t_b", [128, 2 * Dm], F32)
    BUF["xnfT_st"] = bsb("xnfT_st_b", [128, 2 * Dm], F32)
    rets = bsb("rets", [128, 8 * 512], BF16)
    gs_sb = bsb("gs_sb", [128, 2 * 512], BF16)
    ms_sb = bsb("ms_sb", [128, 8 * 512], BF16)
    outTb = bsb("outTb", [128, 4096], F32)
    P.wait(SYN, "V")
    for s8 in range(8):
        P.wait(GPS, "T")
        P.wait(GPS, "QG")
        P.wait(GPS, "V")
        ins = GPS.dma_start(out=rets[:].rearrange("p (k m) -> p k m", k=8),
                            in_=bass.AP(retT_d.tensor, s8 * 512,
                                        [[4096, 128], [524288, 8], [1, 512]]))
        P.inc(ins, "AV")
        for oc in range(8):
            P.wait(SYN, "P")
            ins = SYN.dma_start(out=wstA[:, (oc % 2) * 2048:(oc % 2) * 2048 + 1024]
                                .rearrange("p (k m) -> p k m", k=8),
                                in_=wgb[oc])
            P.inc(ins, "WA" if oc % 2 == 0 else "WB")
            P.wait(GPS, "WA" if oc % 2 == 0 else "WB")
            P.wait(GPS, "T")
            ins = GPS.tensor_copy(wbufA[:, (oc % 2) * 1024:(oc % 2) * 1024 + 1024],
                                  wstA[:, (oc % 2) * 2048:(oc % 2) * 2048 + 1024])
            P.inc(ins, "P")
            P.wait(TEN, "P"); P.wait(TEN, "S")
            bq = bank(oc % 2)
            for kc in range(8):
                ins = TEN.matmul(bq[0:128, 0:512],
                                 wbufA[:, (oc % 2) * 1024 + kc * 128:(oc % 2) * 1024 + (kc + 1) * 128],
                                 xnB[:, kc * S + s8 * 512: kc * S + s8 * 512 + 512],
                                 start=(kc == 0), stop=(kc == 7), skip_group_check=True)
            P.inc(ins, "T")
            P.wait(SCA, "T"); P.wait(SCA, "V")
            ins = SCA.activation(gs_sb[:, (oc % 2) * 512:(oc % 2) * 512 + 512], bq[0:128, 0:512],
                                 AF.Silu)
            P.inc(ins, "S")
            P.wait(VEC, "S"); P.wait(VEC, "AV"); P.wait(VEC, "T")
            ins = VEC.tensor_mul(ms_sb[:, oc * 512:(oc + 1) * 512],
                                 rets[:, oc * 512:(oc + 1) * 512],
                                 gs_sb[:, (oc % 2) * 512:(oc % 2) * 512 + 512])
            P.inc(ins, "V")
        for o2 in range(8):
            P.wait(SYN, "P")
            ins = SYN.dma_start(out=wstA[:, (o2 % 2) * 2048:(o2 % 2) * 2048 + 1024]
                                .rearrange("p (k m) -> p k m", k=8),
                                in_=wob[o2])
            P.inc(ins, "WA" if o2 % 2 == 0 else "WB")
            P.wait(GPS, "WA" if o2 % 2 == 0 else "WB")
            P.wait(GPS, "T")
            ins = GPS.tensor_copy(wbufA[:, (o2 % 2) * 1024:(o2 % 2) * 1024 + 1024],
                                  wstA[:, (o2 % 2) * 2048:(o2 % 2) * 2048 + 1024])
            P.inc(ins, "P")
            P.wait(TEN, "P"); P.wait(TEN, "V")
            bq = bank(2 + o2 % 2)
            for kc in range(8):
                ins = TEN.matmul(bq[0:128, 0:512],
                                 wbufA[:, (o2 % 2) * 1024 + kc * 128:(o2 % 2) * 1024 + (kc + 1) * 128],
                                 ms_sb[:, kc * 512:(kc + 1) * 512],
                                 start=(kc == 0), stop=(kc == 7), skip_group_check=True)
            P.inc(ins, "T")
            P.wait(VEC, "T")
            ins = VEC.tensor_copy(outTb[:, o2 * 512:(o2 + 1) * 512], bq[0:128, 0:512])
            P.inc(ins, "V")
        emit_boundary_strip(None, 0, s8, R2,
                            lambda c, j: outTb[:, c * 512 + j * 128: c * 512 + (j + 1) * 128],
                            R1, xnT4, True, True, 4, 4, 6, gate_dst=wg_d2, gateb_dst=wgb_d2)
    nc.all_engine_barrier()
    bctx.close()
    if phases <= 3:
        _finish(nc, P, ctx, dbg, scr, out, x_in, upto=3)
        return nc, ctx

    emit_ffn(2, xnT4, wg_d2, R1, out, None, True)
    _finish(nc, P, ctx, dbg, scr, out, x_in, upto=4)
    return nc, ctx



PHASE_TENSORS = {0: ["xnT1"], 1: ["R1", "xnT2", "wg"], 2: ["R2", "xnT3"],
                 3: ["R3", "xnT4", "wg2"]}


def _finish(nc, P, ctx, dbg, scr, out, x_in, upto):
    GPS = nc.gpsimd
    GPS.wait_ge(P.sem["QG"], P.base["QG"])
    GPS.wait_ge(P.sem["QW"], P.base["QW"])
    nc.all_engine_barrier()
    avail = sum((PHASE_TENSORS.get(ph, []) for ph in range(upto + 1)), [])
    if os.environ.get("MK_NODBG"):
        avail = []
    for nm, dst in dbg.items():
        if nm in avail:
            ins = GPS.dma_start(out=dst, in_=scr[nm])
            P.inc(ins, "O")
    if upto < 4:
        ins = GPS.dma_start(out=out[0:128, :], in_=x_in[0:128, :])
        P.inc(ins, "O")
    GPS.wait_ge(P.sem["O"], P.base["O"])
    ctx.close()


def _build_twice(debug=0, phases=9):
    global _USED
    _USED = set()
    build(debug, phases)
    nc, ctx = build(debug, phases)
    return nc


def kernel(**inputs):
    packs = host_pack(inputs)
    nc = _build_twice(debug=0, phases=9)
    x = inputs["L0_attn_a_x"]
    in_maps = []
    for c in range(8):
        m = dict(packs)
        m["x"] = np.ascontiguousarray(np.asarray(x[c], dtype=np.float32))
        in_maps.append(m)
    res = run_bass_kernel_spmd(nc, in_maps, core_ids=list(range(8)))
    return np.stack([np.asarray(res.results[c]["out"]) for c in range(8)], axis=0)




# version 02
# speedup vs baseline: 1.2431x; 1.2431x over previous
import os
import numpy as np
from contextlib import ExitStack

import concourse.bass as bass
import concourse.mybir as mybir
from concourse.bass_utils import run_bass_kernel_spmd

F32 = mybir.dt.float32
BF16 = mybir.dt.bfloat16
AF = mybir.ActivationFunctionType
ALU = mybir.AluOpType
AX = mybir.AxisListType
ts = bass.ts

S = 4096
Dm = 1024
NT = 32
NCH = 8
NHEAD = 16
DH = 64
TS = 1024
NSTRIP = S // TS
HR = 3072
HS = 2048
NE = 12
NEG = -1.0e30

_USED = set()


def host_pack(inp):
    P = {}
    C = np.ascontiguousarray

    def lhs_blocks(wT, no):
        K, M = wT.shape
        return C(wT.reshape(K // 128, 128, no, M // no).transpose(2, 1, 0, 3))

    P["wqa"] = C(inp["L0_attn_a_Wq"].T)
    P["wka"] = C(inp["L0_attn_a_Wk"].T)
    P["wva"] = C(inp["L0_attn_a_Wv"].T)
    P["woa"] = lhs_blocks(C(inp["L0_attn_a_Wo"].T), 8)
    for nm, key in [("wqb", "L0_attn_b_wq"), ("wkb", "L0_attn_b_wk"), ("wvb", "L0_attn_b_wv"),
                    ("wgb", "L0_attn_b_wg"), ("wob", "L0_attn_b_wo")]:
        P[nm] = lhs_blocks(C(inp[key].T), 8)
    ws1_ = lhs_blocks(inp["L0_ffn_Ws1"], 32)
    ws2_ = lhs_blocks(inp["L0_ffn_Ws2"], 8)
    we1_ = C(inp["L0_ffn_We1"].reshape(NE, 8, 128, 48, 128).transpose(0, 3, 2, 1, 4))
    perm24 = [(k % 12) * 2 + k // 12 for k in range(24)]
    we2_ = C(inp["L0_ffn_We2"].reshape(NE, 24, 128, 8, 128)[:, perm24]
             .transpose(0, 3, 2, 1, 4))
    P["wffn"] = C(np.concatenate([we1_.ravel(), we2_.ravel(), ws1_.ravel(), ws2_.ravel()]))
    P["wr"] = C(inp["L0_ffn_Wr"])
    P["brr"] = C(inp["L0_ffn_br"].reshape(1, 12))
    P["bs1a"] = C(inp["L0_ffn_bs1"][:HS].reshape(16, 128).T)
    P["bs1g"] = C(inp["L0_ffn_bs1"][HS:].reshape(16, 128).T)
    P["bs2r"] = C(inp["L0_ffn_bs2"].reshape(1, Dm))
    a_ = inp["L0_ffn_be1"][:, :HR].reshape(NE, 24, 128).transpose(0, 2, 1)
    g_ = inp["L0_ffn_be1"][:, HR:].reshape(NE, 24, 128).transpose(0, 2, 1)
    be = np.empty((NE, 128, 48), np.float32)
    for sl in range(2):
        for j in range(12):
            be[:, :, sl * 24 + j * 2 + 0] = a_[:, :, 2 * j + sl]
            be[:, :, sl * 24 + j * 2 + 1] = g_[:, :, 2 * j + sl]
    P["be1ag"] = C(be)
    P["be2all"] = C(np.concatenate([inp["L0_ffn_be2"], inp["L0_ffn_bs2"].reshape(1, Dm)], 0).astype(np.float32))


    f32 = np.float32
    inv_freq = (1.0 / 10000.0 ** (np.arange(0, DH, 2, dtype=f32) / f32(DH))).astype(f32)
    t = np.arange(S, dtype=f32)
    freqs = np.einsum("i,j->ij", t, inv_freq).astype(f32)
    emb = np.concatenate([freqs, freqs], -1)
    P["ropec"] = C(np.cos(emb).astype(f32).T)
    sinm = np.sin(emb).astype(f32).T.copy()
    sinm[:32, :] *= -1.0
    P["ropes"] = C(sinm)
    gamma = (1.0 - np.exp2(-5.0 - np.arange(NHEAD, dtype=f32))).astype(f32)
    ii = np.arange(128, dtype=np.float64)
    dtt = np.zeros((128, NHEAD * 128), f32)
    gup = np.zeros((64, NHEAD * 128), f32)
    gkd = np.zeros((128, NHEAD), f32)
    gcc = np.zeros((64, NHEAD), f32)
    for h in range(NHEAD):
        g = np.float64(gamma[h])
        diff = ii[None, :] - ii[:, None]
        dtt[:, h * 128:(h + 1) * 128] = np.where(diff >= 0, g ** np.maximum(diff, 0) / 8.0, 0.0).astype(f32)
        gup[:, h * 128:(h + 1) * 128] = (g ** (ii[None, :] + 1.0)).astype(f32)
        gkd[:, h] = (g ** (127.0 - ii) / 8.0).astype(f32)
        gcc[:, h] = f32(g ** 128.0)
    P["dtt"] = dtt
    P["gup"] = gup
    P["gkd"] = gkd
    P["gcc"] = gcc
    return P


class Prog:
    KEYS = ("QW", "QG", "QV", "XN", "WY", "XP", "WA", "WB", "RT", "GR", "BE", "AV",
            "T", "V", "S", "P", "O")
    DMA_KEYS = ("QW", "QG", "QV", "XN", "WY", "XP", "WA", "WB", "RT", "GR", "BE", "AV", "O")

    def __init__(self, nc, ctx):
        self.nc = nc
        self.ctx = ctx
        self.sem = {k: ctx.enter_context(nc.semaphore("s" + k)) for k in self.KEYS}
        self.base = {k: 0 for k in self.KEYS}
        self.stack = []
        self.ctr = {}
        self.sreg = {}
        self.engs = {}

    def register_engines(self, engs):
        for eng in engs:
            et = eng.engine
            self.engs[et] = eng
            self.sreg[et] = [eng.alloc_register(f"off{j}_{et}") for j in range(2)]
        for (et, k) in sorted(_USED, key=str):
            eng = self.engs[et]
            r = eng.alloc_register(f"c{k}_{et}")
            eng.reg_mov(r, 0)
            self.ctr[(et, k)] = {"reg": r, "lin": 0}

    def L(self, key):
        v = self.base[key]
        for Lp in self.stack:
            v += Lp.local[key]
        return v

    def inc(self, ins, key, n=None):
        n = n if n is not None else (16 if key in self.DMA_KEYS else 1)
        ins.then_inc(self.sem[key], n)
        if self.stack:
            self.stack[-1].local[key] += n
        else:
            self.base[key] += n
        return ins

    def wait(self, eng, key, skip=0):
        if (eng.engine, key) not in self.ctr:
            _USED.add((eng.engine, key))
            r = eng.alloc_register(f"c{key}_{eng.engine}_x")
            eng.reg_mov(r, 0)
            self.ctr[(eng.engine, key)] = {"reg": r, "lin": self.L(key)}
        c = self.ctr[(eng.engine, key)]
        tgt = self.L(key) - skip
        d = tgt - c["lin"]
        if d > 0:
            eng.reg_add(c["reg"], c["reg"], d)
            c["lin"] = tgt
        eng.wait_ge(self.sem[key], c["reg"])

    def dyn(self, eng, tensor, var, coeff, const, pattern):
        if isinstance(var, int):
            const += var * coeff
            var = None
        if var is None:
            return bass.AP(tensor, const, pattern)
        et = eng.engine
        r = self.sreg[et][self._srot.setdefault(et, 0)]
        self._srot[et] = (self._srot[et] + 1) % 2
        h = _var_handle(var, et)
        eng.reg_mov(r, h)
        if coeff != 1:
            eng.reg_mul(r, r, coeff)
        if const:
            eng.reg_add(r, r, const)
        return bass.AP(tensor, r, pattern)

    _srot = {}

    def dyn2(self, eng, tensor, var1, c1, var2, c2, const, pattern):
        if isinstance(var1, int):
            return self.dyn(eng, tensor, var2, c2, const + var1 * c1, pattern)
        if isinstance(var2, int):
            return self.dyn(eng, tensor, var1, c1, const + var2 * c2, pattern)
        et = eng.engine
        r = self.sreg[et][self._srot.setdefault(et, 0)]
        self._srot[et] = (self._srot[et] + 1) % 2
        eng.reg_mov(r, _var_handle(var1, et))
        if c1 != 1:
            eng.reg_mul(r, r, c1)
        r2 = self.sreg[et][self._srot[et]]
        eng.reg_mov(r2, _var_handle(var2, et))
        if c2 != 1:
            eng.reg_mul(r2, r2, c2)
        eng.reg_add(r, r, r2)
        if const:
            eng.reg_add(r, r, const)
        return bass.AP(tensor, r, pattern)

    def loop(self, name, n):
        return _Loop(self, name, n)


def _var_handle(rv, et):
    val = rv.val
    hs = getattr(val, "handles", None)
    if hs is None:
        raise RuntimeError("dyn() needs a raw loop var")
    for h in hs:
        if h.engine == et:
            return h
    raise RuntimeError(f"no handle for {et}")


class _Loop:
    def __init__(self, prog, name, n):
        self.prog = prog
        self.name = name
        self.n = n
        self.local = {k: 0 for k in Prog.KEYS}
        self.touched = set()
        self.entry_lin = {}
        self.cm = None

    def __enter__(self):
        P = self.prog
        nc = P.nc
        self.entryL = {k: P.L(k) for k in Prog.KEYS}
        for (et, k), c in P.ctr.items():
            d = self.entryL[k] - c["lin"]
            assert d >= 0
            if d:
                P.engs[et].reg_add(c["reg"], c["reg"], d)
                c["lin"] = self.entryL[k]
        import concourse.mybir as _mb
        loop_id = nc.next_id()
        ln_ = f"{self.name}_fori_{loop_id}"
        self.loop_start = f"{ln_}_loop"
        self.loop_end = f"{ln_}_end"
        self.regs = nc.alloc_registers(f"{ln_}_i", engines=_mb.ALL_ENGINES)
        nc.regs_mov(self.regs, 0)
        nc.br(self.loop_start, engines=_mb.ALL_ENGINES)
        self.body_cm = nc.body(self.loop_start, valid_engines=_mb.ALL_ENGINES)
        self.body_cm.__enter__()
        self.var = nc.snap(self.regs, min_val=0, max_val=self.n - 1)
        P.stack.append(self)
        return self.var

    def _close_fori(self):
        P = self.prog
        nc = P.nc
        import concourse.mybir as _mb
        nc.regs_alu(self.regs, self.regs, 1, op=_mb.AluOpType.add)
        nc.br_lt(self.regs, self.n, on_true=self.loop_start, on_false=self.loop_end,
                 engines=_mb.ALL_ENGINES)
        self.body_cm.__exit__(None, None, None)
        nc.switch_bb(self.loop_end)
        for rset in (self.regs, getattr(self.var.val, "handles", None) or ()):
            hs = getattr(rset, "handles", None)
            hs = hs() if callable(hs) else (hs if hs is not None else rset)
            for h in hs:
                try:
                    P.engs[h.engine].free_register(h)
                except Exception:
                    pass

    def __exit__(self, *a):
        P = self.prog
        for (et, k), c in sorted(P.ctr.items(), key=str):
            end = self.entryL[k] + self.local[k]
            adj = end - c["lin"]
            assert adj >= 0, (self.name, et, k, adj)
            if adj:
                P.engs[et].reg_add(c["reg"], c["reg"], adj)
            c["lin"] = end
        P.stack.pop()
        tgt = P.stack[-1].local if P.stack else P.base
        for k, v in self.local.items():
            tgt[k] = tgt[k] + v * self.n
        for (et, k), c in P.ctr.items():
            c["lin"] += (self.n - 1) * self.local[k]
        self._close_fori()
        return False


def build(debug=0, phases=9):
    pass
    nc = bass.Bass("TRN2", target_bir_lowering=False)
    ctx = ExitStack()
    sb = lambda name, shape, dt: ctx.enter_context(nc.sbuf_tensor(name, shape, dt))

    def din(name, shape, dt=F32):
        return nc.dram_tensor(name, list(shape), dt, kind="ExternalInput").ap()

    def dscr(name, shape, dt):
        return nc.dram_tensor(name, list(shape), dt, kind="Internal").ap()

    def dout(name, shape, dt=F32):
        return nc.dram_tensor(name, list(shape), dt, kind="ExternalOutput").ap()

    x_in = din("x", [S, Dm])
    wqa = din("wqa", [64, 64]); wka = din("wka", [64, 64]); wva = din("wva", [64, 64])
    woa = din("woa", [8, 128, 8, 128])
    wqb = din("wqb", [8, 128, 8, 128]); wkb = din("wkb", [8, 128, 8, 128])
    wvb = din("wvb", [8, 128, 8, 128]); wgb = din("wgb", [8, 128, 8, 128])
    wob = din("wob", [8, 128, 8, 128])
    NW1 = NE * 48 * 128 * 1024
    NW2 = NE * 8 * 128 * 3072
    NS1 = 32 * 128 * 1024
    NS2 = 8 * 128 * 2048
    wffn = din("wffn", [NW1 + NW2 + NS1 + NS2])
    wr_in = din("wr", [Dm, 12]); brr = din("brr", [1, 12])
    bs1a = din("bs1a", [128, 16]); bs1g = din("bs1g", [128, 16]); bs2r = din("bs2r", [1, Dm])
    be1ag = din("be1ag", [NE, 128, 48]); be2all = din("be2all", [NE + 1, Dm])
    ropec_in = din("ropec", [64, S]); ropes_in = din("ropes", [64, S])
    dtt_in = din("dtt", [128, NHEAD * 128]); gup_in = din("gup", [64, NHEAD * 128])
    gkd_in = din("gkd", [128, NHEAD]); gcc_in = din("gcc", [64, NHEAD])

    out = dout("out", [S, Dm])

    R1 = dscr("R1", [S, Dm], F32)
    R2 = dscr("R2", [S, Dm], F32)
    xnT1 = dscr("xnT1", [NCH, 128, S], BF16)
    xnT2 = dscr("xnT2", [NCH, 128, S], BF16)
    xnT3 = dscr("xnT3", [NCH, 128, S], BF16)
    xnT4 = dscr("xnT4", [NCH, 128, S], BF16)
    retT_d = dscr("retT_d", [NCH, 128, S], BF16)
    yr_d = dscr("yr_d", [NSTRIP, 128, 8 * TS], F32)
    wffnb_d = dscr("wffnb_d", [NW1 + NW2 + NS1 + NS2], BF16)
    avT_d = dscr("avT_d", [NCH, 128, S], BF16)
    augrow_d = dscr("augrow_d", [2, 32, 128], BF16)
    wg_d = dscr("wg_d", [16, S], F32)
    wg_d2 = dscr("wg_d2", [16, S], F32)
    wgb_d = dscr("wgb_d", [16, S], BF16)
    wgb_d2 = dscr("wgb_d2", [16, S], BF16)

    scr = dict(xnT1=xnT1, R1=R1, xnT2=xnT2, wg=wg_d, R2=R2, xnT3=xnT3, R3=R1, xnT4=xnT4,
               wg2=wg_d2)
    dbg = {}
    if debug:

        dbg["xnT1"] = dout("dbg_xnT1", [NCH, 128, S], BF16)
        dbg["R1"] = dout("dbg_R1", [S, Dm])
        dbg["xnT2"] = dout("dbg_xnT2", [NCH, 128, S], BF16)
        dbg["wg"] = dout("dbg_wg", [16, S])
        dbg["R2"] = dout("dbg_R2", [S, Dm])
        dbg["xnT3"] = dout("dbg_xnT3", [NCH, 128, S], BF16)
        dbg["wg2"] = dout("dbg_wg2", [16, S])
        dbg["R3"] = dout("dbg_R3", [S, Dm])
        dbg["xnT4"] = dout("dbg_xnT4", [NCH, 128, S], BF16)

    TEN, VEC, SCA, GPS, SYN = nc.tensor, nc.vector, nc.scalar, nc.gpsimd, nc.sync
    P = Prog(nc, ctx)
    P.register_engines([TEN, VEC, SCA, GPS, SYN])

    PB = ctx.enter_context(nc.psum_tensor("PB", [128, 4096], F32))

    def bank(i, w=1):
        return PB[:, 512 * i: 512 * (i + w)]

    def bankb(i, w=1):
        return PB[:, 512 * i: 512 * (i + w)].bitcast(BF16)

    identb = sb("identb", [128, 128], BF16)
    identf = sb("identf", [128, 128], F32)
    maskA = sb("maskA", [128, 128], F32)
    maskMb = sb("maskMb", [128, 128], BF16)
    eps_c = sb("eps_c", [128, 1], F32)
    onesc_f = sb("onesc_f", [1, 128], F32)
    iota16 = sb("iota16", [128, 16], F32)
    wr_sb = sb("wr_sb", [128, 8 * 12], F32)
    br_sb = sb("br_sb", [1, 12], F32)
    Rt = sb("Rt", [128, 4 * Dm], F32)
    sqb = sb("sqb", [128, Dm], F32)
    stt = sb("stt", [128, 4 * 16], F32)
    xnb_t = sb("xnb_t", [128, 2 * Dm], BF16)
    xnT_st = sb("xnT_st", [128, 4 * Dm], BF16)
    gstg2 = sb("gstg2", [16, 4 * 128], F32)
    gstg2b = sb("gstg2b", [16, 4 * 128], BF16)
    cvb = sb("cvb", [128, 2 * 2048], BF16)
    BUF = {}
    lgt = sb("lgt", [128, 2 * 16], F32)
    m8b = sb("m8b", [128, 2 * 8], F32)
    gtok = sb("gtok", [128, 2 * 16], F32)
    gmsk = sb("gmsk", [128, 2 * 16], F32)
    gstg = sb("gstg", [16, 2 * 128], F32)
    wstA = sb("wstA", [128, 2 * 2048], F32)
    wbufAG = sb("wbufAG", [128, 2 * 2048], BF16)
    wbufA = wbufAG
    wbufG = None
    bstg = sb("bstg", [128, 64], F32)

    for tgt_, fill, pat, cmpo, cm, fv in [
        (identb, 1.0, [[1, 128]], ALU.is_equal, -1, 0.0),
        (identf, 1.0, [[1, 128]], ALU.is_equal, -1, 0.0),
        (maskA, 0.0, [[-1, 128]], ALU.is_ge, 1, NEG),
        (maskMb, 1.0, [[1, 128]], ALU.is_ge, -1, 0.0),
    ]:
        ins = GPS.memset(tgt_[:], fill)
        P.inc(ins, "P")
        P.wait(GPS, "P")
        ins = GPS.affine_select(tgt_[:], tgt_[:], pattern=pat, compare_op=cmpo, fill=fv,
                                base=0, channel_multiplier=cm)
        P.inc(ins, "P")
    ins = GPS.iota(iota16[:], pattern=[[1, 16]], base=0, channel_multiplier=0,
                   allow_small_or_imprecise_dtypes=True)
    P.inc(ins, "P")
    VEC.memset(eps_c[:], 1e-5)
    VEC.memset(onesc_f[:], 1.0)
    VEC.memset(lgt[:], NEG)
    ins = VEC.memset(stt[:], 0.0)
    P.inc(ins, "V")
    ins = SYN.dma_start(out=wr_sb[:].rearrange("p (k m) -> p k m", k=8),
                        in_=wr_in.rearrange("(k p) m -> p k m", p=128))
    P.inc(ins, "QW")
    ins = SYN.dma_start(out=br_sb[:], in_=brr)
    P.inc(ins, "QW")
    P.wait(GPS, "V"); P.wait(GPS, "P")
    P.wait(TEN, "V"); P.wait(TEN, "P")
    P.wait(VEC, "P")
    P.wait(SCA, "V"); P.wait(SCA, "P")

    def emit_boundary_strip(svar, svc, svk, R_src, outT_src, R_dst, xnT_dst, router, do_ln,
                            b_trf, b_trb, b_rt, gate_dst=None, r_dst_is_out=False,
                            gateb_dst=None, gbias=None):
        rpat4 = [[1024, 128], [131072, 4], [1, 1024]]
        P.wait(GPS, "V")
        P.wait(GPS, "QG")
        P.wait(GPS, "O")
        P.wait(VEC, "QG")
        ins = GPS.dma_start(out=Rt[:].rearrange("p (t m) -> p t m", t=4),
                            in_=P.dyn(GPS, R_src.tensor, svar, svc * 524288, svk * 524288, rpat4))
        P.inc(ins, "RT")
        P.wait(VEC, "RT")
        P.wait(SCA, "RT")
        for j in range(4):
            o = j * 16
            rt = Rt[:, j * Dm:(j + 1) * Dm]
            sq = sqb[:, 0:Dm]
            if outT_src is not None:
                for g in range(2):
                    P.wait(TEN, "V")
                    for jj in range(4):
                        c = g * 4 + jj
                        pt = bank(b_trf + g)[:, jj * 128:(jj + 1) * 128]
                        ins = TEN.transpose(pt, outT_src(c, j), identf[:])
                    P.inc(ins, "T")
                    P.wait(VEC, "T")
                    ins = VEC.tensor_add(rt[:, g * 512:(g + 1) * 512], rt[:, g * 512:(g + 1) * 512],
                                         bank(b_trf + g))
                    P.inc(ins, "V")
                if gbias is not None:
                    wgb13, be2b = gbias
                    for g in range(2):
                        P.wait(TEN, "V")
                        ins = TEN.matmul(bank(b_rt)[:, 0:512], wgb13[0:13, j * 128:(j + 1) * 128],
                                         be2b[0:13, g * 512:(g + 1) * 512], start=True, stop=True,
                                         skip_group_check=True)
                        P.inc(ins, "T")
                        P.wait(VEC, "T")
                        ins = VEC.tensor_add(rt[:, g * 512:(g + 1) * 512],
                                             rt[:, g * 512:(g + 1) * 512], bank(b_rt)[:, 0:512])
                        P.inc(ins, "V")
            if not do_ln:
                continue
            if outT_src is not None:
                P.wait(VEC, "V")
                P.wait(SCA, "V")
            ins = VEC.reduce_sum(stt[:, o:o + 1], rt, axis=AX.X)
            P.inc(ins, "V")
            ins = SCA.square(sq, rt)
            P.inc(ins, "S")
            P.wait(VEC, "S")
            ins = VEC.reduce_sum(stt[:, o + 1:o + 2], sq, axis=AX.X)
            P.inc(ins, "V")
            P.wait(VEC, "V")
            VEC.tensor_scalar(stt[:, o + 2:o + 3], stt[:, o:o + 1], 1.0 / Dm, None, ALU.mult)
            ins = VEC.tensor_scalar(stt[:, o + 3:o + 4], stt[:, o + 1:o + 2], 1.0 / Dm, None, ALU.mult)
            P.inc(ins, "V")
            P.wait(VEC, "V")
            ins = VEC.tensor_mul(stt[:, o + 5:o + 6], stt[:, o + 2:o + 3], stt[:, o + 2:o + 3])
            P.inc(ins, "V")
            P.wait(VEC, "V")
            ins = VEC.tensor_sub(stt[:, o + 4:o + 5], stt[:, o + 3:o + 4], stt[:, o + 5:o + 6])
            P.inc(ins, "V")
            P.wait(SCA, "V")
            ins = SCA.activation(stt[:, o + 6:o + 7], stt[:, o + 4:o + 5], AF.Ln, bias=eps_c[:, 0:1])
            P.inc(ins, "S")
            P.wait(SCA, "S")
            ins = SCA.activation(stt[:, o + 7:o + 8], stt[:, o + 6:o + 7], AF.Exp, scale=-0.5)
            P.inc(ins, "S")
            P.wait(VEC, "S")
            ins = VEC.tensor_scalar(xnb_t[:, (j % 2) * Dm:(j % 2 + 1) * Dm], rt,
                                    stt[:, o + 2:o + 3], stt[:, o + 7:o + 8], ALU.subtract, ALU.mult)
            P.inc(ins, "V")
            if router:
                ins = VEC.tensor_scalar(BUF["xnf_t"][:, (j % 2) * Dm:(j % 2 + 1) * Dm], rt,
                                        stt[:, o + 2:o + 3], stt[:, o + 7:o + 8],
                                        ALU.subtract, ALU.mult)
                P.inc(ins, "V")
            P.wait(TEN, "V")
            for g in range(2):
                if g:
                    P.wait(TEN, "V")
                for jj in range(4):
                    c = g * 4 + jj
                    pt = bankb(b_trb + g)[:, jj * 128:(jj + 1) * 128]
                    ins = TEN.transpose(pt, xnb_t[:, (j % 2) * Dm + c * 128:(j % 2) * Dm + (c + 1) * 128],
                                        identb[:])
                P.inc(ins, "T")
                P.wait(VEC, "T")
                dst = xnT_st[:].rearrange("p (c m) -> p c m", c=8)[:, 4 * g:4 * g + 4,
                                                                  j * 128:(j + 1) * 128]
                ins = VEC.tensor_copy(dst,
                                      bankb(b_trb + g)[:, 0:512].rearrange("p (c m) -> p c m", c=4))
                P.inc(ins, "V")
            if not router:
                continue
            for g in range(2):
                P.wait(TEN, "V")
                for jj in range(4):
                    c = g * 4 + jj
                    pt = bank(b_trf + g)[:, jj * 128:(jj + 1) * 128]
                    ins = TEN.transpose(pt, BUF["xnf_t"][:, (j % 2) * Dm + c * 128:
                                                         (j % 2) * Dm + (c + 1) * 128], identf[:])
                P.inc(ins, "T")
                P.wait(VEC, "T")
                ins = VEC.tensor_copy(BUF["xnfT_st"][:, (j % 2) * Dm + g * 512:
                                                     (j % 2) * Dm + (g + 1) * 512],
                                      bank(b_trf + g))
                P.inc(ins, "V")
            P.wait(TEN, "V")
            rpb = bank(b_rt)[:, 0:16]
            for c in range(NCH):
                TEN.matmul(rpb[:, 0:12],
                           BUF["xnfT_st"][:, (j % 2) * Dm + c * 128:(j % 2) * Dm + (c + 1) * 128],
                           wr_sb[:, c * 12:(c + 1) * 12], start=(c == 0), stop=False)
            ins = TEN.matmul(rpb[:, 0:12], onesc_f[:], br_sb[:], start=False, stop=True)
            P.inc(ins, "T")
            P.wait(VEC, "T")
            ins = VEC.tensor_scalar(lgt[:, (j % 2) * 16:(j % 2) * 16 + 12], rpb[:, 0:12],
                                    1.0 / 1.5, None, ALU.mult)
            P.inc(ins, "V")
            o2 = (j % 2) * 16
            P.wait(VEC, "V")
            ins = VEC.max(m8b[:, (j % 2) * 8:(j % 2 + 1) * 8], lgt[:, o2:o2 + 16])
            P.inc(ins, "V")
            m1 = m8b[:, (j % 2) * 8 + 0:(j % 2) * 8 + 1]
            m2 = m8b[:, (j % 2) * 8 + 1:(j % 2) * 8 + 2]
            P.wait(VEC, "V")
            ins = VEC.tensor_sub(stt[:, o + 8:o + 9], m1, m2)
            P.inc(ins, "V")
            P.wait(SCA, "V")
            SCA.activation(stt[:, o + 9:o + 10], stt[:, o + 8:o + 9], AF.Sigmoid)
            ins = SCA.activation(stt[:, o + 10:o + 11], stt[:, o + 8:o + 9], AF.Sigmoid, scale=-1.0)
            P.inc(ins, "S")
            ins = VEC.tensor_scalar(gmsk[:, o2:o2 + 16], lgt[:, o2:o2 + 16], m1, 1000.0,
                                    ALU.not_equal, ALU.mult)
            P.inc(ins, "V")
            P.wait(VEC, "V")
            ins = VEC.tensor_add(gmsk[:, o2:o2 + 16], gmsk[:, o2:o2 + 16], iota16[:])
            P.inc(ins, "V")
            P.wait(VEC, "V")
            ins = VEC.tensor_reduce(stt[:, o + 11:o + 12], gmsk[:, o2:o2 + 16], axis=AX.X, op=ALU.min)
            P.inc(ins, "V")
            P.wait(VEC, "V")
            ins = VEC.tensor_scalar(gmsk[:, o2:o2 + 16], lgt[:, o2:o2 + 16], m2, 1000.0,
                                    ALU.not_equal, ALU.mult)
            P.inc(ins, "V")
            P.wait(VEC, "V")
            ins = VEC.tensor_add(gmsk[:, o2:o2 + 16], gmsk[:, o2:o2 + 16], iota16[:])
            P.inc(ins, "V")
            P.wait(VEC, "V")
            ins = VEC.tensor_scalar(gtok[:, o2:o2 + 16], iota16[:], stt[:, o + 11:o + 12], 1000.0,
                                    ALU.is_equal, ALU.mult)
            P.inc(ins, "V")
            P.wait(VEC, "V")
            ins = VEC.tensor_add(gmsk[:, o2:o2 + 16], gmsk[:, o2:o2 + 16], gtok[:, o2:o2 + 16])
            P.inc(ins, "V")
            P.wait(VEC, "V")
            ins = VEC.tensor_reduce(stt[:, o + 12:o + 13], gmsk[:, o2:o2 + 16], axis=AX.X, op=ALU.min)
            P.inc(ins, "V")
            P.wait(VEC, "V"); P.wait(VEC, "S")
            ins = VEC.tensor_scalar(gmsk[:, o2:o2 + 16], iota16[:], stt[:, o + 11:o + 12],
                                    stt[:, o + 9:o + 10], ALU.is_equal, ALU.mult)
            P.inc(ins, "V")
            ins = VEC.tensor_scalar(gtok[:, o2:o2 + 16], iota16[:], stt[:, o + 12:o + 13],
                                    stt[:, o + 10:o + 11], ALU.is_equal, ALU.mult)
            P.inc(ins, "V")
            P.wait(VEC, "V")
            ins = VEC.tensor_add(gtok[:, o2:o2 + 16], gtok[:, o2:o2 + 16], gmsk[:, o2:o2 + 16])
            P.inc(ins, "V")
            P.wait(VEC, "V")
            ins = VEC.memset(gtok[:, o2 + 12:o2 + 13], 1.0)
            P.inc(ins, "V")
            P.wait(TEN, "V")
            gpb = bank(b_trf)[:, 0:128]
            ins = TEN.transpose(gpb[0:16, :], gtok[:, o2:o2 + 16], identf[:])
            P.inc(ins, "T")
            P.wait(VEC, "T")
            ins = VEC.tensor_copy(gstg2[:, j * 128:(j + 1) * 128], gpb[0:16, :])
            P.inc(ins, "V")
            ins = VEC.tensor_copy(gstg2b[:, j * 128:(j + 1) * 128], gpb[0:16, :])
            P.inc(ins, "V")
        if R_dst is not None:
            P.wait(GPS, "V")
            ins = GPS.dma_start(out=P.dyn(GPS, R_dst.tensor, svar, svc * 524288, svk * 524288,
                                          rpat4),
                                in_=Rt[:].rearrange("p (t m) -> p t m", t=4))
            P.inc(ins, "O" if r_dst_is_out else "QG")
        if do_ln:
            P.wait(GPS, "V")
            ins = GPS.dma_start(
                out=P.dyn(GPS, xnT_dst.tensor, svar, svc * 512, svk * 512,
                          [[4096, 128], [524288, 8], [1, 512]]),
                in_=xnT_st[:].rearrange("p (c m) -> p c m", c=8))
            P.inc(ins, "QG")
        if router:
            P.wait(GPS, "V")
            ins = GPS.dma_start(out=P.dyn(GPS, gate_dst.tensor, svar, svc * 512, svk * 512,
                                          [[4096, 16], [1, 512]]),
                                in_=gstg2[:])
            P.inc(ins, "QG")
            ins = GPS.dma_start(out=P.dyn(GPS, gateb_dst.tensor, svar, svc * 512, svk * 512,
                                          [[4096, 16], [1, 512]]),
                                in_=gstg2b[:])
            P.inc(ins, "QG")

    if os.environ.get("MK_EMPTY"):
        _finish(nc, P, ctx, dbg, scr, out, x_in, upto=0 if not os.environ.get("MK_NODBG") else -1)
        return nc, ctx
    def emit_prepass(pname, src_h, dst_h, n_units):
        with P.loop("cvt" + pname, n_units) as uu:
            P.wait(SYN, "P")
            ins = SYN.dma_start(out=wstA[:, 0:2048],
                                in_=P.dyn(SYN, src_h, uu, 262144, 0,
                                          [[2048, 128], [1, 2048]]))
            P.inc(ins, "WA")
            P.wait(GPS, "WA")
            P.wait(GPS, "QG")
            ins = GPS.tensor_copy(cvb[:, 0:2048], wstA[:, 0:2048])
            P.inc(ins, "P")
            ins = GPS.dma_start(out=P.dyn(GPS, dst_h, uu, 262144, 0,
                                          [[2048, 128], [1, 2048]]),
                                in_=cvb[:, 0:2048])
            P.inc(ins, "QG")

    emit_prepass("a", wffn.tensor, wffnb_d.tensor, (NW1 + NW2 + NS1 + NS2) // 262144)

    for tt_s in range(NT // 4):
        emit_boundary_strip(None, 0, tt_s, x_in, None, None, xnT1, False, True, 0, 6, 5)
    nc.all_engine_barrier()
    if phases <= 0:
        _finish(nc, P, ctx, dbg, scr, out, x_in, upto=0)
        return nc, ctx

    pctx = ExitStack()
    psb = lambda name, shape, dt: pctx.enter_context(nc.sbuf_tensor(name, shape, dt))
    xnp = psb("xnp", [128, 2 * S], BF16)
    qTa = psb("qTa", [66, S], BF16)
    kTa = psb("kTa", [66, S], BF16)
    v_sb = psb("v_sb", [128, NT * 64], BF16)
    s_row = psb("s_row", [128, S], F32)
    Pdump = psb("Pdump", [128, 1024], BF16)
    sump = psb("sump", [128, 8], F32)
    augc = psb("augc", [128, 8], F32)
    augH = psb("augH", [128, 32], BF16)
    augL = psb("augL", [128, 32], BF16)
    augsb = psb("augsb", [32, 2 * 128], BF16)
    PT_sb = psb("PT_sb", [128, 2 * 512], BF16)
    wqa_f = psb("wqa_f", [64, 3 * 64], F32)
    wqa_b = psb("wqa_b", [128, 3 * 64], BF16)
    BUF["xnf_t"] = psb("xnf_t_a", [128, 2 * Dm], F32)
    BUF["xnfT_st"] = psb("xnfT_st_a", [128, 2 * Dm], F32)

    ins = SYN.dma_start(out=wqa_f[:, 0:64], in_=wqa); P.inc(ins, "QW")
    ins = SYN.dma_start(out=wqa_f[:, 64:128], in_=wka); P.inc(ins, "QW")
    ins = SYN.dma_start(out=wqa_f[:, 128:192], in_=wva); P.inc(ins, "QW")
    P.wait(VEC, "QW")
    VEC.tensor_copy(wqa_b[0:64, :], wqa_f[:])
    ins = VEC.tensor_copy(wqa_b[64:128, :], wqa_f[:]); P.inc(ins, "V")
    ins = VEC.memset(kTa[64:66, :], 1.0); P.inc(ins, "V")
    avh_st = psb("avh_st", [128, S], BF16)
    avs_sb = psb("avs_sb", [128, 8 * 512], BF16)
    P.wait(GPS, "QG")

    with P.loop("ata", 4) as tt4:
      for par in range(2):
        pp = tt4 * 2 + par
        P.wait(GPS, "T")
        ins = GPS.dma_start(
            out=xnp[:, par * S:(par + 1) * S],
            in_=P.dyn(GPS, xnT1.tensor, tt4, 2 * 524288, par * 524288,
                      [[4096, 128], [1, 4096]]))
        P.inc(ins, "XP")
        for half in range(2):
            bp = half * 64
            P.wait(TEN, "XP")
            P.wait(TEN, "V")
            for st8 in range(8):
                bq = bank(st8 % 2)
                P.wait(TEN, "S")
                ins = TEN.matmul(bq[0:64, 0:512], wqa_b[bp:bp + 64, 0:64],
                                 xnp[bp:bp + 64, par * S + st8 * 512: par * S + (st8 + 1) * 512],
                                 start=True, stop=True)
                P.inc(ins, "T")
                P.wait(SCA, "T")
                ins = SCA.activation(qTa[0:64, st8 * 512:(st8 + 1) * 512], bq[0:64, 0:512], AF.Copy)
                P.inc(ins, "S")
            for st8 in range(8):
                bq = bank(2 + st8 % 2)
                P.wait(TEN, "S")
                ins = TEN.matmul(bq[0:64, 0:512], wqa_b[bp:bp + 64, 64:128],
                                 xnp[bp:bp + 64, par * S + st8 * 512: par * S + (st8 + 1) * 512],
                                 start=True, stop=True)
                P.inc(ins, "T")
                P.wait(SCA, "T")
                ins = SCA.activation(kTa[0:64, st8 * 512:(st8 + 1) * 512], bq[0:64, 0:512], AF.Copy)
                P.inc(ins, "S")
            for tt8 in range(NT):
                bq = bank(4 + tt8 % 2)
                P.wait(TEN, "V")
                ins = TEN.matmul(bq[0:128, 0:64],
                                 xnp[bp:bp + 64, par * S + tt8 * 128: par * S + (tt8 + 1) * 128],
                                 wqa_b[bp:bp + 64, 128:192], start=True, stop=True)
                P.inc(ins, "T")
                P.wait(VEC, "T")
                ins = VEC.tensor_copy(v_sb[:, tt8 * 64:(tt8 + 1) * 64], bq[0:128, 0:64])
                P.inc(ins, "V")
            for qi in range(NT):
                kvalid = (qi + 1) * 128
                nks = (kvalid + 511) // 512
                P.wait(TEN, "S")
                for ks in range(nks):
                    N = min(512, kvalid - ks * 512)
                    bq = bank(ks % 2)
                    if ks >= 2:
                        P.wait(TEN, "S")
                    ins = TEN.matmul(bq[0:128, 0:N], qTa[0:64, qi * 128:(qi + 1) * 128],
                                     kTa[0:64, ks * 512:ks * 512 + N], start=True, stop=True)
                    P.inc(ins, "T")
                    P.wait(SCA, "T")
                    ins = SCA.activation(s_row[:, ks * 512:ks * 512 + N], bq[0:128, 0:N],
                                         AF.Copy, scale=0.125)
                    P.inc(ins, "S")
                P.wait(VEC, "S")
                ins = VEC.tensor_add(s_row[:, qi * 128:kvalid], s_row[:, qi * 128:kvalid], maskA[:])
                P.inc(ins, "V")
                P.wait(VEC, "V")
                ins = VEC.reduce_max(augc[:, 0:1], s_row[:, 0:kvalid], axis=AX.X, negate=True)
                P.inc(ins, "V")
                P.wait(SCA, "V")
                nch_ = (kvalid + 1023) // 1024
                for j in range(nch_):
                    w = min(1024, kvalid - j * 1024)
                    if j:
                        P.wait(SCA, "S")
                    ins = SCA.activation(Pdump[:, 0:w], s_row[:, j * 1024:j * 1024 + w], AF.Exp,
                                         bias=augc[:, 0:1], accum_out=sump[:, j:j + 1])
                    P.inc(ins, "S")
                P.wait(VEC, "S")
                ins = VEC.reduce_sum(augc[:, 1:2], sump[:, 0:nch_], axis=AX.X)
                P.inc(ins, "V")
                P.wait(SCA, "V")
                ins = SCA.activation(augc[:, 2:3], augc[:, 1:2], AF.Ln)
                P.inc(ins, "S")
                P.wait(VEC, "S")
                ins = VEC.tensor_sub(augc[:, 3:4], augc[:, 0:1], augc[:, 2:3])
                P.inc(ins, "V")
                P.wait(VEC, "V")
                ins = VEC.tensor_scalar(augc[:, 4:5], augc[:, 3:4], 8.0, None, ALU.mult)
                P.inc(ins, "V")
                P.wait(VEC, "V")
                ins = VEC.tensor_copy(augH[:, qi:qi + 1], augc[:, 4:5])
                P.inc(ins, "V")
                P.wait(VEC, "V")
                ins = VEC.tensor_sub(augL[:, qi:qi + 1], augc[:, 4:5], augH[:, qi:qi + 1])
                P.inc(ins, "V")
            P.wait(TEN, "V")
            ins = TEN.transpose(bankb(0)[0:32, 0:128], augH[:], identb[:])
            P.inc(ins, "T")
            ins = TEN.transpose(bankb(1)[0:32, 0:128], augL[:], identb[:])
            P.inc(ins, "T")
            P.wait(VEC, "T")
            ins = VEC.tensor_copy(augsb[:, 0:128], bankb(0)[0:32, 0:128])
            P.inc(ins, "V")
            ins = VEC.tensor_copy(augsb[:, 128:256], bankb(1)[0:32, 0:128])
            P.inc(ins, "V")
            P.wait(GPS, "V")
            ins = GPS.dma_start(out=augrow_d[0], in_=augsb[:, 0:128])
            P.inc(ins, "QG")
            ins = GPS.dma_start(out=augrow_d[1], in_=augsb[:, 128:256])
            P.inc(ins, "QG")
            P.wait(GPS, "QG")
            ins = GPS.dma_start(out=qTa[64:65, :], in_=augrow_d[0].rearrange("p m -> (p m)").unsqueeze(0))
            P.inc(ins, "QG")
            ins = GPS.dma_start(out=qTa[65:66, :], in_=augrow_d[1].rearrange("p m -> (p m)").unsqueeze(0))
            P.inc(ins, "QG")
            P.wait(TEN, "QG")
            for qs in range(8):
                nkj = 4 * (qs + 1)
                for kj in range(nkj):
                    bq = bank(kj % 2)
                    P.wait(TEN, "S")
                    if kj >= 4 * qs:
                        P.wait(TEN, "V")
                    ins = TEN.matmul(bq[0:128, 0:512], kTa[0:66, kj * 128:(kj + 1) * 128],
                                     qTa[0:66, qs * 512:(qs + 1) * 512], start=True, stop=True,
                                     skip_group_check=True)
                    P.inc(ins, "T")
                    P.wait(SCA, "T")
                    psl = PT_sb[:, (kj % 2) * 512:(kj % 2) * 512 + 512]
                    ins = SCA.activation(psl, bq[0:128, 0:512], AF.Exp, scale=0.125)
                    P.inc(ins, "S")
                    jloc = kj - 4 * qs
                    if jloc >= 0:
                        P.wait(VEC, "S")
                        if jloc > 0:
                            ins = VEC.memset(psl[:, 0:jloc * 128], 0.0)
                            P.inc(ins, "V")
                        ins = VEC.tensor_mul(psl[:, jloc * 128:(jloc + 1) * 128],
                                             psl[:, jloc * 128:(jloc + 1) * 128], maskMb[:])
                        P.inc(ins, "V")
                        P.wait(TEN, "V")
                    else:
                        P.wait(TEN, "S")
                    ins = TEN.matmul(bank(4)[0:64, 0:512], v_sb[:, kj * 64:(kj + 1) * 64], psl,
                                     start=(kj == 0), stop=(kj == nkj - 1), skip_group_check=True)
                    P.inc(ins, "T")
                P.wait(VEC, "T")
                P.wait(VEC, "QG")
                ins = VEC.tensor_copy(avh_st[bp:bp + 64, qs * 512:(qs + 1) * 512],
                                      bank(4)[0:64, 0:512])
                P.inc(ins, "V")
                P.wait(TEN, "V")
        P.wait(GPS, "V")
        ins = GPS.dma_start(out=P.dyn(GPS, avT_d.tensor, tt4, 2 * 524288, par * 524288,
                                      [[4096, 128], [1, 4096]]),
                            in_=avh_st[:])
        P.inc(ins, "QG")
    outT_strip = s_row
    P.wait(SYN, "V")
    for s8 in range(NSTRIP * (TS // 512)):
        P.wait(GPS, "T")
        P.wait(GPS, "QG")
        ins = GPS.dma_start(out=avs_sb[:].rearrange("p (k m) -> p k m", k=8),
                            in_=bass.AP(avT_d.tensor, s8 * 512,
                                        [[4096, 128], [524288, 8], [1, 512]]))
        P.inc(ins, "AV")
        for oc in range(8):
            P.wait(SYN, "P")
            ins = SYN.dma_start(out=wstA[:, (oc % 2) * 2048:(oc % 2) * 2048 + 1024]
                                .rearrange("p (k m) -> p k m", k=8),
                                in_=woa[oc].rearrange("p k m -> p k m"))
            P.inc(ins, "WA" if oc % 2 == 0 else "WB")
            P.wait(GPS, "WA" if oc % 2 == 0 else "WB")
            P.wait(GPS, "T")
            ins = GPS.tensor_copy(wbufA[:, (oc % 2) * 1024:(oc % 2) * 1024 + 1024],
                                  wstA[:, (oc % 2) * 2048:(oc % 2) * 2048 + 1024])
            P.inc(ins, "P")
            P.wait(TEN, "P")
            P.wait(TEN, "V")
            bq = bank(oc % 2)
            for c in range(NCH):
                if c == 0:
                    P.wait(TEN, "AV")
                ins = TEN.matmul(bq[0:128, 0:512],
                                 wbufA[:, (oc % 2) * 1024 + c * 128:(oc % 2) * 1024 + (c + 1) * 128],
                                 avs_sb[:, c * 512:(c + 1) * 512], start=(c == 0), stop=(c == 7))
            P.inc(ins, "T")
            P.wait(VEC, "T")
            ins = VEC.tensor_copy(outT_strip[:, oc * 512:(oc + 1) * 512], bq[0:128, 0:512])
            P.inc(ins, "V")
        emit_boundary_strip(None, 0, s8, x_in,
                            lambda c, j: outT_strip[:, c * 512 + j * 128: c * 512 + (j + 1) * 128],
                            R1, xnT2, True, True, 2, 2, 5, gate_dst=wg_d, gateb_dst=wgb_d)
    nc.all_engine_barrier()
    pctx.close()
    if phases <= 1:
        _finish(nc, P, ctx, dbg, scr, out, x_in, upto=1)
        return nc, ctx

    def emit_ffn(inv, xnT_src, gate_src, R_src, R_dst, xnT_dst, final):
        STG = GPS if inv == 1 else VEC
        STGK = "P" if inv == 1 else "V"
        YST = GPS if inv == 1 else SCA
        YSTK = "QG" if inv == 1 else "QV"
        fctx = ExitStack()
        fsb = lambda name, shape, dt: fctx.enter_context(nc.sbuf_tensor(name + str(inv), shape, dt))
        xnS = fsb("xnS", [128, 8 * TS], BF16)
        act_sb = fsb("act_sb", [128, 24 * TS], BF16)
        yr = fsb("yr", [128, 8 * TS], F32)
        wstY = fsb("wstY", [128, 1536], F32)
        wbufY = fsb("wbufY", [128, 2 * 3072], BF16)
        wbc_sb = fsb("wbc_sb", [128, TS], F32)
        tmpa = fsb("tmpa", [128, 2 * TS], BF16)
        tmpg = fsb("tmpg", [128, 2 * TS], BF16)
        be1ag_sb = fsb("be1ag_sb", [128, 48], F32)
        bs1c = fsb("bs1c", [128, 32], F32)
        wgb13 = fsb("wgb13", [16, 512], BF16)
        be2b = fsb("be2b", [16, Dm], BF16)
        bias_cur = fsb("bias_cur", [128, 2 * 2], F32)
        grow_f = fsb("grow_f", [1, TS], F32)
        grow_b = fsb("grow_b", [1, TS], BF16)

        P.wait(SYN, "QG")
        ins = SYN.dma_start(out=bs1c[:, 0:16], in_=bs1a); P.inc(ins, "QW")
        ins = SYN.dma_start(out=bs1c[:, 16:32], in_=bs1g); P.inc(ins, "QW")
        P.wait(SYN, "P"); P.wait(SYN, "T")
        ins = SYN.dma_start(out=wstA[0:13, 0:Dm], in_=be2all)
        P.inc(ins, "QW")
        P.wait(VEC, "QW")
        ins = VEC.tensor_copy(be2b[0:13, :], wstA[0:13, 0:Dm])
        P.inc(ins, "V")
        P.wait(SYN, "V")

        def act_stage(nf, lhs_dyn, bias_a, bias_g, gated):
            pass

        with P.loop(f"ffnS{inv}", NSTRIP) as sv:
            P.wait(GPS, "T")
            P.wait(GPS, "QG")
            ins = GPS.dma_start(out=xnS[:].rearrange("p (c m) -> p c m", c=8),
                                in_=P.dyn(GPS, xnT_src.tensor, sv, TS, 0,
                                          [[4096, 128], [524288, 8], [1, TS]]))
            P.inc(ins, "XN")
            P.wait(VEC, "T")
            P.wait(VEC, "V")
            P.wait(VEC, YSTK)
            ins = VEC.memset(yr[:], 0.0)
            P.inc(ins, "V")



            def emit_mlp(nf, we_dyn_a, bias_a_ap, bias_g_ap, gated, ksrc_n, aug_lhs, aug_rhs,
                         y_dyn, loopname):
                pass

            NEX = int(os.environ.get("MK_NEX", NE))
            with P.loop(f"ffnE{inv}", NEX) as ev:
                P.wait(SCA, "T")
                P.wait(SCA, "V")
                ins = SCA.dma_start(out=grow_f[:],
                                    in_=P.dyn2(SCA, gate_src.tensor, ev, S, sv, TS, 0,
                                               [[1, 1], [1, TS]]))
                P.inc(ins, "GR")
                ins = SCA.dma_start(out=be1ag_sb[:],
                                    in_=P.dyn(SCA, be1ag.tensor, ev, 128 * 48, 0,
                                              [[48, 128], [1, 48]]))
                P.inc(ins, "BE")

                P.wait(SCA, "GR"); P.wait(SCA, "BE")
                ins = SCA.nop()
                P.inc(ins, "S")
                P.wait(VEC, "S")
                P.wait(VEC, "T")
                ins = VEC.tensor_copy(grow_b[:], grow_f[:])
                P.inc(ins, "V")
                P.wait(TEN, "S"); P.wait(TEN, "V")
                TEN.matmul(bank(4)[:, 0:512], onesc_f[:], grow_f[0:1, 0:512], start=True, stop=True)
                ins = TEN.matmul(bank(5)[:, 0:512], onesc_f[:], grow_f[0:1, 512:1024],
                                 start=True, stop=True)
                P.inc(ins, "T")
                P.wait(VEC, "T")
                VEC.tensor_copy(wbc_sb[:, 0:512], bank(4)[:, 0:512])
                ins = VEC.tensor_copy(wbc_sb[:, 512:1024], bank(5)[:, 0:512])
                P.inc(ins, "V")

                with P.loop(f"ffnF{inv}", 12) as fv:
                    for sl in range(2):
                        P.wait(SYN, "T", skip=2)
                        ins = SYN.dma_start(
                            out=wbufAG[:, sl * 2048:(sl + 1) * 2048].rearrange(
                                "p (h m) -> p h m", h=2),
                            in_=P.dyn2(SYN, wffnb_d.tensor, ev, 48 * 131072, fv, 2 * 131072,
                                       sl * 131072,
                                       [[1024, 128], [24 * 131072, 2], [1, 1024]]))
                        P.inc(ins, "WA" if sl == 0 else "WB")
                        P.wait(TEN, "WA" if sl == 0 else "WB")
                        P.wait(TEN, "P"); P.wait(TEN, "V", skip=2); P.wait(TEN, "S", skip=1)
                        P.wait(TEN, "XN")
                        pa = bank(0 + 2 * sl, 2)
                        pg = bank(4 + 2 * sl, 2)
                        for st2 in range(2):
                            for k in range(8):
                                ins = TEN.matmul(
                                    pa[:, st2 * 512:(st2 + 1) * 512],
                                    wbufAG[:, sl * 2048 + k * 128: sl * 2048 + (k + 1) * 128],
                                    xnS[:, k * TS + st2 * 512: k * TS + st2 * 512 + 512],
                                    start=(k == 0), stop=(k == 7), skip_group_check=True)
                        P.inc(ins, "T")
                        for st2 in range(2):
                            for k in range(8):
                                ins = TEN.matmul(
                                    pg[:, st2 * 512:(st2 + 1) * 512],
                                    wbufAG[:, sl * 2048 + 1024 + k * 128:
                                            sl * 2048 + 1024 + (k + 1) * 128],
                                    xnS[:, k * TS + st2 * 512: k * TS + st2 * 512 + 512],
                                    start=(k == 0), stop=(k == 7), skip_group_check=True)
                        P.inc(ins, "T")
                        P.wait(STG, "S"); P.wait(STG, "V")
                        bsl = be1ag_sb[:, sl * 24:(sl + 1) * 24][:, ts(fv, 2)]
                        ins = STG.tensor_copy(bias_cur[:, sl * 2:(sl + 1) * 2], bsl)
                        P.inc(ins, STGK)
                        P.wait(VEC, "T", skip=1)
                        P.wait(VEC, STGK)
                        ins = VEC.scalar_tensor_tensor(tmpa[:, sl * TS:(sl + 1) * TS], pa,
                                                       bias_cur[:, sl * 2:sl * 2 + 1],
                                                       wbc_sb[:], ALU.add, ALU.mult)
                        P.inc(ins, "V")
                        P.wait(SCA, "T")
                        P.wait(SCA, STGK)
                        ins = SCA.activation(tmpg[:, sl * TS:(sl + 1) * TS], pg, AF.Gelu,
                                             bias=bias_cur[:, sl * 2 + 1:sl * 2 + 2])
                        P.inc(ins, "S")
                        P.wait(VEC, "S"); P.wait(VEC, "V")
                        P.wait(VEC, "T")
                        ins = VEC.tensor_mul(act_sb[:, sl * 12 * TS:(sl + 1) * 12 * TS][:, ts(fv, TS)],
                                             tmpa[:, sl * TS:(sl + 1) * TS],
                                             tmpg[:, sl * TS:(sl + 1) * TS])
                        P.inc(ins, "V")
                with P.loop(f"ffnY{inv}", 4 if not os.environ.get("MK_NOY") else 1) as ov:
                    for po in range(2):
                        P.wait(SYN, "T", skip=1)
                        ins = SYN.dma_start(
                            out=wbufY[:, po * 3072:(po + 1) * 3072],
                            in_=P.dyn2(SYN, wffnb_d.tensor, ev, 8 * 393216, ov, 2 * 393216,
                                       NW1 + po * 393216,
                                       [[3072, 128], [1, 3072]]))
                        P.inc(ins, "WY")
                        py = bank(0 + 2 * po, 2)
                        P.wait(TEN, "WY")
                        P.wait(TEN, "V")
                        P.wait(TEN, "S")
                        for st2 in range(2):
                            for k in range(24):
                                ins = TEN.matmul(py[:, st2 * 512:(st2 + 1) * 512],
                                           wbufY[:, po * 3072 + k * 128: po * 3072 + (k + 1) * 128],
                                           act_sb[:, k * TS + st2 * 512: k * TS + st2 * 512 + 512],
                                           start=(k == 0),
                                           stop=(k == 23), skip_group_check=True)
                        P.inc(ins, "T")
                        P.wait(VEC, "T")
                        ydst = yr[:, po * 4 * TS:(po + 1) * 4 * TS][:, ts(ov, TS)]
                        ins = VEC.tensor_add(ydst, ydst, py[:, 0:1024])
                        P.inc(ins, "V")
            for fc in range(16):
                sl = fc % 2
                P.wait(SYN, "T", skip=2)
                ins = SYN.dma_start(
                    out=wbufAG[:, sl * 2048:(sl + 1) * 2048].rearrange("p (h m) -> p h m", h=2),
                    in_=bass.AP(wffnb_d.tensor, NW1 + NW2 + fc * 131072,
                                [[1024, 128], [16 * 131072, 2], [1, 1024]]))
                P.inc(ins, "WA" if sl == 0 else "WB")
                P.wait(TEN, "WA" if sl == 0 else "WB")
                P.wait(TEN, "P"); P.wait(TEN, "V", skip=2); P.wait(TEN, "S", skip=1)
                P.wait(TEN, "XN")
                pa = bank(0 + 2 * sl, 2)
                pg = bank(4 + 2 * sl, 2)
                for st2 in range(2):
                    for k in range(8):
                        ins = TEN.matmul(pa[:, st2 * 512:(st2 + 1) * 512],
                                         wbufAG[:, sl * 2048 + k * 128: sl * 2048 + (k + 1) * 128],
                                         xnS[:, k * TS + st2 * 512: k * TS + st2 * 512 + 512],
                                         start=(k == 0), stop=(k == 7), skip_group_check=True)
                P.inc(ins, "T")
                for st2 in range(2):
                    for k in range(8):
                        ins = TEN.matmul(pg[:, st2 * 512:(st2 + 1) * 512],
                                         wbufAG[:, sl * 2048 + 1024 + k * 128:
                                                 sl * 2048 + 1024 + (k + 1) * 128],
                                         xnS[:, k * TS + st2 * 512: k * TS + st2 * 512 + 512],
                                         start=(k == 0), stop=(k == 7), skip_group_check=True)
                P.inc(ins, "T")
                P.wait(VEC, "T")
                ins = VEC.tensor_scalar(tmpa[:, sl * TS:(sl + 1) * TS], pa,
                                        bs1c[:, fc:fc + 1], None, ALU.add)
                P.inc(ins, "V")
                P.wait(SCA, "T")
                ins = SCA.activation(tmpg[:, sl * TS:(sl + 1) * TS], pg, AF.Gelu,
                                     bias=bs1c[:, 16 + fc:17 + fc])
                P.inc(ins, "S")
                P.wait(VEC, "S"); P.wait(VEC, "V")
                P.wait(VEC, "T")
                ins = VEC.tensor_mul(act_sb[:, fc * TS:(fc + 1) * TS],
                                     tmpa[:, sl * TS:(sl + 1) * TS],
                                     tmpg[:, sl * TS:(sl + 1) * TS])
                P.inc(ins, "V")
            for oc in range(8):
                po = oc % 2
                P.wait(SYN, "T", skip=1)
                ins = SYN.dma_start(out=wbufY[:, po * 3072: po * 3072 + 2048],
                                    in_=bass.AP(wffnb_d.tensor, NW1 + NW2 + NS1 + oc * 262144,
                                                [[2048, 128], [1, 2048]]))
                P.inc(ins, "WY")
                py = bank(0 + 2 * po, 2)
                P.wait(TEN, "WY")
                P.wait(TEN, "P"); P.wait(TEN, "V", skip=1)
                for st2 in range(2):
                    for k in range(16):
                        ins = TEN.matmul(py[:, st2 * 512:(st2 + 1) * 512],
                                   wbufY[:, po * 3072 + k * 128: po * 3072 + (k + 1) * 128],
                                   act_sb[:, k * TS + st2 * 512: k * TS + st2 * 512 + 512],
                                   start=(k == 0), stop=(k == 15), skip_group_check=True)
                P.inc(ins, "T")
                P.wait(VEC, "T")
                ypos = (oc % 2) * 4 + oc // 2
                ins = VEC.tensor_add(yr[:, ypos * TS:(ypos + 1) * TS],
                                     yr[:, ypos * TS:(ypos + 1) * TS], py[:, 0:1024])
                P.inc(ins, "V")

            P.wait(YST, "V"); P.wait(YST, "P")
            ins = YST.dma_start(out=P.dyn(YST, yr_d.tensor, sv, 128 * 8 * TS, 0,
                                          [[8 * TS, 128], [1, 8 * TS]]),
                                in_=yr[:])
            P.inc(ins, YSTK)
        gateb_src = wgb_d if inv == 1 else wgb_d2
        for s8 in range(8):
            P.wait(GPS, "T")
            P.wait(GPS, YSTK)
            P.wait(SYN, "T")
            ins = SYN.dma_start(out=wgb13[:], in_=bass.AP(gateb_src.tensor, s8 * 512,
                                                          [[4096, 16], [1, 512]]))
            P.inc(ins, "WY")
            P.wait(TEN, "WY")
            base_ = (s8 // 2) * 128 * 8 * TS + (s8 % 2) * 512
            ins = GPS.dma_start(
                out=yr[:, 0:4096].rearrange("p (c m) -> p c m", c=8)[:, 0:8:2, :],
                in_=bass.AP(yr_d.tensor, base_, [[8 * TS, 128], [TS, 4], [1, 512]]))
            P.inc(ins, "AV")
            ins = GPS.dma_start(
                out=yr[:, 0:4096].rearrange("p (c m) -> p c m", c=8)[:, 1:8:2, :],
                in_=bass.AP(yr_d.tensor, base_ + 4 * TS, [[8 * TS, 128], [TS, 4], [1, 512]]))
            P.inc(ins, "AV")
            P.wait(TEN, "AV")
            emit_boundary_strip(
                None, 0, s8, R_src,
                lambda c, j: yr[:, c * 512 + j * 128: c * 512 + (j + 1) * 128],
                R_dst, xnT_dst, False, not final, 0, 2, 5,
                r_dst_is_out=final, gbias=(wgb13, be2b))
        nc.all_engine_barrier()
        fctx.close()

    emit_ffn(1, xnT2, wg_d, R1, R2, xnT3, False)
    if phases <= 2:
        _finish(nc, P, ctx, dbg, scr, out, x_in, upto=2)
        return nc, ctx

    bctx = ExitStack()
    bsb = lambda name, shape, dt: bctx.enter_context(nc.sbuf_tensor(name, shape, dt))
    xnB = bsb("xnB", [128, NCH * S], BF16)
    gkd_sb = bsb("gkd_sb", [128, NHEAD], F32)
    gcc_sb = bsb("gcc_sb", [64, NHEAD], F32)
    rT_st = bsb("rT_st", [128, S], BF16)
    rctx = ExitStack()
    rsb = lambda name, shape, dt: rctx.enter_context(nc.sbuf_tensor(name, shape, dt))
    ropecb = rsb("ropecb", [64, S], BF16)
    ropesb = rsb("ropesb", [64, S], BF16)
    dttb = rsb("dttb", [128, NHEAD * 128], BF16)
    gupb = rsb("gupb", [64, NHEAD * 128], BF16)
    qTb = rsb("qTb", [64, S], BF16)
    kTb = rsb("kTb", [64, S], BF16)
    vTb = rsb("vTb", [64, S], BF16)
    qsf = rsb("qsf", [64, 512], F32)
    qsw = rsb("qsw", [64, 512], F32)
    qt1 = rsb("qt1", [64, 512], F32)
    v_tok = rsb("v_tok", [128, 2 * 64], BF16)
    ktil = rsb("ktil", [128, 2 * 64], BF16)
    qtl = rsb("qtl", [64, 2 * 128], BF16)
    PT_r = rsb("PT_r", [128, 2 * 128], BF16)
    S_sb = rsb("S_sb", [64, 2 * 64], BF16)
    wbufV = rsb("wbufV", [128, 1024], BF16)
    dcur = rsb("dcur", [128, 2 * 128], BF16)
    gcur = rsb("gcur", [64, 2 * 128], BF16)
    kdcur = rsb("kdcur", [128, 2], F32)
    ccur = rsb("ccur", [64, 2], F32)

    P.wait(GPS, "QG")
    ins = GPS.dma_start(out=xnB[:].rearrange("p (c m) -> p c m", c=8),
                        in_=xnT3.rearrange("c p m -> p c m"))
    P.inc(ins, "QG")
    P.wait(SYN, "P"); P.wait(SYN, "T")
    ins = SYN.dma_start(out=wstA[0:64, 0:4096], in_=ropec_in)
    P.inc(ins, "QW")
    ins = SYN.dma_start(out=wstA[64:128, 0:4096], in_=ropes_in)
    P.inc(ins, "QW")
    P.wait(VEC, "QW")
    VEC.tensor_copy(ropecb[:], wstA[0:64, 0:4096])
    ins = VEC.tensor_copy(ropesb[:], wstA[64:128, 0:4096])
    P.inc(ins, "V")
    P.wait(SYN, "V")
    ins = SYN.dma_start(out=wstA[:, 0:2048], in_=dtt_in)
    P.inc(ins, "QW")
    ins = SYN.dma_start(out=wstA[0:64, 2048:4096], in_=gup_in)
    P.inc(ins, "QW")
    ins = SYN.dma_start(out=gkd_sb[:], in_=gkd_in)
    P.inc(ins, "QW")
    ins = SYN.dma_start(out=gcc_sb[:], in_=gcc_in)
    P.inc(ins, "QW")
    P.wait(VEC, "QW")
    VEC.tensor_copy(dttb[:], wstA[:, 0:2048])
    ins = VEC.tensor_copy(gupb[:], wstA[0:64, 2048:4096])
    P.inc(ins, "V")

    with P.loop("atb", 4) as bb4:
      for par in range(2):
        pp = bb4 * 2 + par
        hexp = pp * 2
        P.wait(SYN, "V")
        for wsrc, key, dst in [(wqb, "WA", wbufAG[:, 0:1024]), (wkb, "WB", wbufAG[:, 2048:3072]),
                               (wvb, "WY", wbufV[:, 0:1024])]:
            P.wait(SYN, "P")
            ins = SYN.dma_start(out=wstA[:, 0:1024],
                                in_=P.dyn(SYN, wsrc.tensor, bb4, 2 * 131072, par * 131072,
                                          [[1024, 128], [1, 1024]]))
            P.inc(ins, key)
            P.wait(GPS, key)
            P.wait(GPS, "T")
            ins = GPS.tensor_copy(dst, wstA[:, 0:1024])
            P.inc(ins, "P")
        P.wait(SCA, "V"); P.wait(SCA, "QW")
        SCA.copy(dcur[:], dttb[:, ts(pp, 256)])
        SCA.copy(gcur[:], gupb[:, ts(pp, 256)])
        SCA.copy(kdcur[:], gkd_sb[:, ts(pp, 2)])
        ins = SCA.copy(ccur[:], gcc_sb[:, ts(pp, 2)])
        P.inc(ins, "S")
        for half in range(2):
            bp = half * 64
            P.wait(TEN, "QG")
            for which, wbuf_, dstT, do_rope in [(0, wbufAG[:, 0:1024], qTb, True),
                                                (1, wbufAG[:, 2048:3072], kTb, True),
                                                (2, wbufV[:, 0:1024], vTb, False)]:
                for st8 in range(8):
                    bq = bank(st8 % 2)
                    P.wait(TEN, "P"); P.wait(TEN, "V"); P.wait(TEN, "S")
                    for kc in range(8):
                        ins = TEN.matmul(bq[0:64, 0:512],
                                         wbuf_[:, kc * 128 + bp: kc * 128 + bp + 64],
                                         xnB[:, kc * S + st8 * 512: kc * S + st8 * 512 + 512],
                                         start=(kc == 0), stop=(kc == 7), skip_group_check=True)
                    P.inc(ins, "T")
                    if not do_rope:
                        P.wait(SCA, "T")
                        ins = SCA.activation(dstT[0:64, st8 * 512:(st8 + 1) * 512],
                                             bq[0:64, 0:512], AF.Copy)
                        P.inc(ins, "S")
                    else:
                        P.wait(VEC, "T")
                        P.wait(VEC, "V")
                        ins = VEC.tensor_copy(qsf[:], bq[0:64, 0:512])
                        P.inc(ins, "V")
                        P.wait(VEC, "V")
                        VEC.tensor_copy(qsw[0:32, :], qsf[32:64, :])
                        ins = VEC.tensor_copy(qsw[32:64, :], qsf[0:32, :])
                        P.inc(ins, "V")
                        P.wait(VEC, "V")
                        ins = VEC.tensor_mul(qt1[:], qsf[:], ropecb[:, st8 * 512:(st8 + 1) * 512])
                        P.inc(ins, "V")
                        P.wait(VEC, "V")
                        ins = VEC.tensor_mul(qsf[:], qsw[:], ropesb[:, st8 * 512:(st8 + 1) * 512])
                        P.inc(ins, "V")
                        P.wait(VEC, "V")
                        ins = VEC.tensor_add(dstT[0:64, st8 * 512:(st8 + 1) * 512], qt1[:], qsf[:])
                        P.inc(ins, "V")
            P.wait(VEC, "QG")
            P.wait(VEC, "S")
            ins = VEC.memset(S_sb[:], 0.0)
            P.inc(ins, "V")
            for kj in range(NT):
                sl = kj % 2
                P.wait(TEN, "S"); P.wait(TEN, "V")
                ins = TEN.matmul(bank(sl)[0:128, 0:128], kTb[:, kj * 128:(kj + 1) * 128],
                                 qTb[:, kj * 128:(kj + 1) * 128], start=True, stop=True,
                                 skip_group_check=True)
                P.inc(ins, "T")
                P.wait(VEC, "T")
                ins = VEC.tensor_mul(PT_r[:, sl * 128:(sl + 1) * 128], bank(sl)[0:128, 0:128],
                                     dcur[:, half * 128:(half + 1) * 128])
                P.inc(ins, "V")
                ins = TEN.transpose(bankb(4)[0:128, 0:64], vTb[:, kj * 128:(kj + 1) * 128],
                                    identb[0:64, 0:64])
                P.inc(ins, "T")
                ins = TEN.transpose(bankb(5)[0:128, 0:64], kTb[:, kj * 128:(kj + 1) * 128],
                                    identb[0:64, 0:64])
                P.inc(ins, "T")
                P.wait(VEC, "T")
                ins = VEC.tensor_copy(v_tok[:, sl * 64:(sl + 1) * 64], bankb(4)[0:128, 0:64])
                P.inc(ins, "V")
                ins = VEC.tensor_scalar(ktil[:, sl * 64:(sl + 1) * 64], bankb(5)[0:128, 0:64],
                                        kdcur[:, half:half + 1], None, ALU.mult)
                P.inc(ins, "V")
                ins = VEC.tensor_mul(qtl[:, sl * 128:(sl + 1) * 128],
                                     qTb[:, kj * 128:(kj + 1) * 128],
                                     gcur[:, half * 128:(half + 1) * 128])
                P.inc(ins, "V")
                P.wait(TEN, "V")
                TEN.matmul(bank(2)[0:64, 0:128], v_tok[:, sl * 64:(sl + 1) * 64],
                           PT_r[:, sl * 128:(sl + 1) * 128], start=True, stop=False,
                           skip_group_check=True)
                ins = TEN.matmul(bank(2)[0:64, 0:128], S_sb[:, (kj % 2) * 64:(kj % 2) * 64 + 64]
                                 if False else S_sb[:, (1 - sl) * 64:(2 - sl) * 64],
                                 qtl[:, sl * 128:(sl + 1) * 128], start=False, stop=True,
                                 skip_group_check=True)
                P.inc(ins, "T")
                ins = TEN.matmul(bank(3)[0:64, 0:64], ktil[:, sl * 64:(sl + 1) * 64],
                                 v_tok[:, sl * 64:(sl + 1) * 64], start=True, stop=True,
                                 skip_group_check=True)
                P.inc(ins, "T")
                P.wait(VEC, "T")
                ins = VEC.tensor_copy(rT_st[bp:bp + 64, kj * 128:(kj + 1) * 128],
                                      bank(2)[0:64, 0:128])
                P.inc(ins, "V")
                ins = VEC.scalar_tensor_tensor(S_sb[:, sl * 64:(sl + 1) * 64],
                                               S_sb[:, (1 - sl) * 64:(2 - sl) * 64],
                                               ccur[:, half:half + 1],
                                               bank(3)[0:64, 0:64], ALU.mult, ALU.add)
                P.inc(ins, "V")
        P.wait(GPS, "V")
        ins = GPS.dma_start(out=P.dyn(GPS, retT_d.tensor, bb4, 2 * 524288, par * 524288,
                                      [[4096, 128], [1, 4096]]),
                            in_=rT_st[:])
        P.inc(ins, "QG")
    nc.all_engine_barrier()
    rctx.close()
    BUF["xnf_t"] = bsb("xnf_t_b", [128, 2 * Dm], F32)
    BUF["xnfT_st"] = bsb("xnfT_st_b", [128, 2 * Dm], F32)
    rets = bsb("rets", [128, 8 * 512], BF16)
    gs_sb = bsb("gs_sb", [128, 2 * 512], BF16)
    ms_sb = bsb("ms_sb", [128, 8 * 512], BF16)
    outTb = bsb("outTb", [128, 4096], F32)
    P.wait(SYN, "V")
    for s8 in range(8):
        P.wait(GPS, "T")
        P.wait(GPS, "QG")
        P.wait(GPS, "V")
        ins = GPS.dma_start(out=rets[:].rearrange("p (k m) -> p k m", k=8),
                            in_=bass.AP(retT_d.tensor, s8 * 512,
                                        [[4096, 128], [524288, 8], [1, 512]]))
        P.inc(ins, "AV")
        for oc in range(8):
            P.wait(SYN, "P")
            ins = SYN.dma_start(out=wstA[:, (oc % 2) * 2048:(oc % 2) * 2048 + 1024]
                                .rearrange("p (k m) -> p k m", k=8),
                                in_=wgb[oc])
            P.inc(ins, "WA" if oc % 2 == 0 else "WB")
            P.wait(GPS, "WA" if oc % 2 == 0 else "WB")
            P.wait(GPS, "T")
            ins = GPS.tensor_copy(wbufA[:, (oc % 2) * 1024:(oc % 2) * 1024 + 1024],
                                  wstA[:, (oc % 2) * 2048:(oc % 2) * 2048 + 1024])
            P.inc(ins, "P")
            P.wait(TEN, "P"); P.wait(TEN, "S")
            bq = bank(oc % 2)
            for kc in range(8):
                ins = TEN.matmul(bq[0:128, 0:512],
                                 wbufA[:, (oc % 2) * 1024 + kc * 128:(oc % 2) * 1024 + (kc + 1) * 128],
                                 xnB[:, kc * S + s8 * 512: kc * S + s8 * 512 + 512],
                                 start=(kc == 0), stop=(kc == 7), skip_group_check=True)
            P.inc(ins, "T")
            P.wait(SCA, "T"); P.wait(SCA, "V")
            ins = SCA.activation(gs_sb[:, (oc % 2) * 512:(oc % 2) * 512 + 512], bq[0:128, 0:512],
                                 AF.Silu)
            P.inc(ins, "S")
            P.wait(VEC, "S"); P.wait(VEC, "AV"); P.wait(VEC, "T")
            ins = VEC.tensor_mul(ms_sb[:, oc * 512:(oc + 1) * 512],
                                 rets[:, oc * 512:(oc + 1) * 512],
                                 gs_sb[:, (oc % 2) * 512:(oc % 2) * 512 + 512])
            P.inc(ins, "V")
        for o2 in range(8):
            P.wait(SYN, "P")
            ins = SYN.dma_start(out=wstA[:, (o2 % 2) * 2048:(o2 % 2) * 2048 + 1024]
                                .rearrange("p (k m) -> p k m", k=8),
                                in_=wob[o2])
            P.inc(ins, "WA" if o2 % 2 == 0 else "WB")
            P.wait(GPS, "WA" if o2 % 2 == 0 else "WB")
            P.wait(GPS, "T")
            ins = GPS.tensor_copy(wbufA[:, (o2 % 2) * 1024:(o2 % 2) * 1024 + 1024],
                                  wstA[:, (o2 % 2) * 2048:(o2 % 2) * 2048 + 1024])
            P.inc(ins, "P")
            P.wait(TEN, "P"); P.wait(TEN, "V")
            bq = bank(2 + o2 % 2)
            for kc in range(8):
                ins = TEN.matmul(bq[0:128, 0:512],
                                 wbufA[:, (o2 % 2) * 1024 + kc * 128:(o2 % 2) * 1024 + (kc + 1) * 128],
                                 ms_sb[:, kc * 512:(kc + 1) * 512],
                                 start=(kc == 0), stop=(kc == 7), skip_group_check=True)
            P.inc(ins, "T")
            P.wait(VEC, "T")
            ins = VEC.tensor_copy(outTb[:, o2 * 512:(o2 + 1) * 512], bq[0:128, 0:512])
            P.inc(ins, "V")
        emit_boundary_strip(None, 0, s8, R2,
                            lambda c, j: outTb[:, c * 512 + j * 128: c * 512 + (j + 1) * 128],
                            R1, xnT4, True, True, 4, 4, 6, gate_dst=wg_d2, gateb_dst=wgb_d2)
    nc.all_engine_barrier()
    bctx.close()
    if phases <= 3:
        _finish(nc, P, ctx, dbg, scr, out, x_in, upto=3)
        return nc, ctx

    emit_ffn(2, xnT4, wg_d2, R1, out, None, True)
    _finish(nc, P, ctx, dbg, scr, out, x_in, upto=4)
    return nc, ctx



PHASE_TENSORS = {0: ["xnT1"], 1: ["R1", "xnT2", "wg"], 2: ["R2", "xnT3"],
                 3: ["R3", "xnT4", "wg2"]}


def _finish(nc, P, ctx, dbg, scr, out, x_in, upto):
    GPS = nc.gpsimd
    GPS.wait_ge(P.sem["QG"], P.base["QG"])
    GPS.wait_ge(P.sem["QW"], P.base["QW"])
    nc.all_engine_barrier()
    avail = sum((PHASE_TENSORS.get(ph, []) for ph in range(upto + 1)), [])
    if os.environ.get("MK_NODBG"):
        avail = []
    for nm, dst in dbg.items():
        if nm in avail:
            ins = GPS.dma_start(out=dst, in_=scr[nm])
            P.inc(ins, "O")
    if upto < 4:
        ins = GPS.dma_start(out=out[0:128, :], in_=x_in[0:128, :])
        P.inc(ins, "O")
    GPS.wait_ge(P.sem["O"], P.base["O"])
    ctx.close()


def _build_twice(debug=0, phases=9):
    global _USED
    _USED = set()
    build(debug, phases)
    nc, ctx = build(debug, phases)
    return nc


def kernel(**inputs):
    packs = host_pack(inputs)
    nc = _build_twice(debug=0, phases=9)
    x = inputs["L0_attn_a_x"]
    in_maps = []
    for c in range(8):
        m = dict(packs)
        m["x"] = np.ascontiguousarray(np.asarray(x[c], dtype=np.float32))
        in_maps.append(m)
    res = run_bass_kernel_spmd(nc, in_maps, core_ids=list(range(8)))
    return np.stack([np.asarray(res.results[c]["out"]) for c in range(8)], axis=0)


